# version 14
# baseline (speedup 1.0000x reference)
import numpy as np
from contextlib import ExitStack

import concourse.bass as bass
import concourse.mybir as mybir
from concourse.bass_utils import run_bass_kernel_spmd

F32 = mybir.dt.float32
BF16 = mybir.dt.bfloat16
AF = mybir.ActivationFunctionType
ALU = mybir.AluOpType
AX = mybir.AxisListType

P = 128
D = 1024
DT = D // P
T = 2048
TH = T // 2
TQ = 512
NQ = T // TQ
O4 = 4096
OT4 = O4 // P
L = 16
CH = TH // L
CHQ = TQ // L
FT = 512
EPS = 1e-5

_PC = {}
_off = 0
for _n, _w in [("Bm", 128), ("Cm", 128), ("Dm", DT), ("lnw", DT), ("lnb", DT),
               ("bout", DT), ("s1", OT4), ("s2", OT4),
               ("s1h", OT4), ("s2h", OT4), ("s3", DT)]:
    _PC[_n] = _off
    _off += _w
NPAR = _off

_DEBUG = []


class Ctr:
    def __init__(self, stack, nc, name):
        self.h = stack.enter_context(nc.semaphore(name))
        self.v = 0

    def inc(self, instr, n=1):
        instr.then_inc(self.h, n)
        self.v += n
        return self.v


class _RecInstr:
    __slots__ = ("incs",)

    def __init__(self):
        self.incs = []

    def then_inc(self, sem, n):
        self.incs.append((sem, n))
        return self


class _Rec:

    def __init__(self):
        self.ops = []
        self.low_precision = [False]

    def last(self):
        return self.ops[-1][3]

    def __getattr__(self, meth):
        if meth.startswith("_"):
            raise AttributeError(meth)
        ops = self.ops
        lp = self.low_precision

        def call(*a, **k):
            proxy = _RecInstr()
            ops.append((meth, a, k, proxy, lp[0]))
            return proxy
        return call


def _replay(nc, rec, eng):
    for meth, a, k, proxy, lp in rec.ops:
        if lp:
            with nc.allow_low_precision(reason="bf16 chunk-state storage"):
                ins = getattr(eng, meth)(*a, **k)
        else:
            ins = getattr(eng, meth)(*a, **k)
        for sem, n in proxy.incs:
            ins.then_inc(sem, n)


def build_nc():
    nc = bass.Bass("TRN2", target_bir_lowering=False, debug=False)

    xt = nc.dram_tensor("xt", [D, T], F32, kind="ExternalInput").ap()
    parA = nc.dram_tensor("parA", [P, DT * 256], F32, kind="ExternalInput").ap()
    par = nc.dram_tensor("par", [P, NPAR], F32, kind="ExternalInput").ap()
    wout_t = nc.dram_tensor("wout_t", [D, D], F32, kind="ExternalInput").ap()
    w1_t = nc.dram_tensor("w1_t", [D, O4], F32, kind="ExternalInput").ap()
    w2_t = nc.dram_tensor("w2_t", [D, O4], F32, kind="ExternalInput").ap()
    h1_t = nc.dram_tensor("h1_t", [O4, O4], F32, kind="ExternalInput").ap()
    h2_t = nc.dram_tensor("h2_t", [O4, O4], F32, kind="ExternalInput").ap()
    w3_t = nc.dram_tensor("w3_t", [O4, D], F32, kind="ExternalInput").ap()

    y_out = nc.dram_tensor("y_out", [D, T], F32, kind="ExternalOutput").ap()
    x1sp = nc.dram_tensor("x1sp", [D, T], F32, kind="ExternalOutput").ap()
    woutb = nc.dram_tensor("woutb", [DT * P, DT * P], BF16,
                           kind="Internal").ap()
    w1b = nc.dram_tensor("w1b", [OT4 * P, DT * P], BF16, kind="Internal").ap()
    w2b = nc.dram_tensor("w2b", [OT4 * P, DT * P], BF16, kind="Internal").ap()
    h1b = nc.dram_tensor("h1b", [OT4 * P, OT4 * P], BF16,
                         kind="Internal").ap()
    h2b = nc.dram_tensor("h2b", [OT4 * P, OT4 * P], BF16,
                         kind="Internal").ap()
    w3b = nc.dram_tensor("w3b", [DT * P, OT4 * P], BF16, kind="Internal").ap()

    dbg = {}
    for name in _DEBUG:
        shape = {"z": [D, T], "a": [D, T], "x1": [D, T],
                 "Y0": [D, TQ], "Y1": [D, TQ], "g10": [D, TQ],
                 "u": [P, DT * 272], "k": [P, DT * 16], "M": [P, DT * 256],
                 "pv": [P, DT * 256],
                 "h0": [O4, TQ], "g0": [O4, TQ], "f0": [D, TQ],
                 "z20": [D, TQ], "Hb0": [P, DT * 528],
                 "Hle0": [P, DT * 528]}[name]
        dt_ = BF16 if name in ("g10", "h0", "g0", "z20", "Hb0",
                               "Hle0", "M", "pv") else F32
        dbg[name] = nc.dram_tensor("dbg_" + name, shape, dt_,
                                   kind="ExternalOutput").ap()

    dbg_env = {}

    def dbg_dump(SP, name, sbuf_ap_3d, ctr_wait):
        if name not in dbg:
            return
        c, v = ctr_wait
        sOUT, V = dbg_env["sOUT"], dbg_env["V"]
        SP.wait_ge(c.h, v)
        ins = SP.dma_start(out=dbg[name].rearrange("(dt p) t -> p dt t", p=P),
                           in_=sbuf_ap_3d)
        done = sOUT.inc(ins, 16)
        V.wait_ge(sOUT.h, done)

    with ExitStack() as top:
        et = top.enter_context
        spar = et(nc.sbuf_tensor("spar", [P, NPAR], F32))
        ones_bc = et(nc.sbuf_tensor("ones_bc", [P, P], BF16))
        epsc = et(nc.sbuf_tensor("epsc", [P, 1], F32))
        mhalf = et(nc.sbuf_tensor("mhalf", [P, 1], F32))
        m_bc = et(nc.sbuf_tensor("m_bc", [P, FT], F32))
        r_bc = et(nc.sbuf_tensor("r_bc", [P, FT], F32))
        bfs = et(nc.sbuf_tensor("bfs", [P, 2 * FT], BF16))
        sqs = et(nc.sbuf_tensor("sqs", [P, 2 * FT], BF16))
        acts = et(nc.sbuf_tensor("acts", [P, 2 * FT], F32))

        pb = [et(nc.psum_tensor(f"pb{i}", [P, FT], F32)) for i in range(4)]
        p_m = et(nc.psum_tensor("p_m", [P, FT], F32))
        p_r = et(nc.psum_tensor("p_r", [P, FT], F32))

        sIN = Ctr(top, nc, "sIN")
        sWD = Ctr(top, nc, "sWD")
        sWB = Ctr(top, nc, "sWB")
        sGP = Ctr(top, nc, "sGP")
        sV = Ctr(top, nc, "sV")
        sS = Ctr(top, nc, "sS")
        sT = Ctr(top, nc, "sT")
        sOUT = Ctr(top, nc, "sOUT")

        SP, GP, V, S, TT = _Rec(), _Rec(), _Rec(), _Rec(), _Rec()
        dbg_env["sOUT"] = sOUT
        dbg_env["V"] = V

        bank_free = [None] * 4
        stats_free = [None]

        def acquire_bank(i):
            if bank_free[i] is not None:
                c, val = bank_free[i]
                TT.wait_ge(c.h, val)

        def pcol(name, i=0, w=1):
            return spar[:, _PC[name] + i:_PC[name] + i + w]

        def red16(eng, out3, tb4):
            if eng.ops is V.ops:
                eng.tensor_reduce(out=out3, in_=tb4, axis=AX.X, op=ALU.add)
                return
            for w2 in (8, 4, 2):
                eng.tensor_tensor(out=tb4[:, :, :, 0:w2],
                                  in0=tb4[:, :, :, 0:w2],
                                  in1=tb4[:, :, :, w2:2 * w2], op=ALU.add)
            eng.tensor_tensor(out=out3.unsqueeze(3), in0=tb4[:, :, :, 0:1],
                              in1=tb4[:, :, :, 1:2], op=ALU.add)

        prev_te_last = [0]

        def emit_ln_stats_ft(src_bf):
            te_done = {}
            if stats_free[0] is not None:
                c, val = stats_free[0]
                TT.wait_ge(c.h, val)
            for dt in range(DT):
                kind, ap = src_bf(dt)
                guard = te_done.get(dt - 2, prev_te_last[0] if dt < 2 else 0)
                if kind == "copy":
                    dst = bfs[:, (dt % 2) * FT:(dt % 2) * FT + FT]
                    if guard:
                        S.wait_ge(sT.h, guard)
                    ins = S.copy(out=dst, in_=ap)
                    s_cp = sS.inc(ins)
                    bf_ap = dst
                    V.wait_ge(sS.h, s_cp)
                else:
                    bf_ap = ap
                sq = sqs[:, (dt % 2) * FT:(dt % 2) * FT + FT]
                if guard:
                    V.wait_ge(sT.h, guard)
                ins = V.tensor_tensor(out=sq, in0=bf_ap, in1=bf_ap,
                                      op=ALU.mult)
                v_sq = sV.inc(ins)
                TT.wait_ge(sV.h, v_sq)
                TT.matmul(out=p_m[:], lhsT=ones_bc[:], rhs=bf_ap,
                          start=(dt == 0), stop=(dt == DT - 1))
                ins = TT.matmul(out=p_r[:], lhsT=ones_bc[:], rhs=sq,
                                start=(dt == 0), stop=(dt == DT - 1))
                te_done[dt] = sT.inc(ins)
            prev_te_last[0] = te_done[DT - 1]
            V.wait_ge(sT.h, te_done[DT - 1])
            V.tensor_scalar(out=m_bc[:], in0=p_m[:], scalar1=1.0 / D,
                            scalar2=None, op0=ALU.mult)
            ins = V.tensor_scalar(out=r_bc[:], in0=p_r[:], scalar1=1.0 / D,
                                  scalar2=None, op0=ALU.mult)
            stats_free[0] = (sV, sV.inc(ins))
            tmpf = sqs[:].bitcast(F32)[:, 0:FT]
            V.tensor_tensor(out=tmpf, in0=m_bc[:], in1=m_bc[:], op=ALU.mult)
            ins = V.tensor_tensor(out=r_bc[:], in0=r_bc[:], in1=tmpf,
                                  op=ALU.subtract)
            v_var = sV.inc(ins)
            S.wait_ge(sV.h, v_var)
            S.activation(out=r_bc[:], in_=r_bc[:], func=AF.Ln, bias=epsc[:])
            ins = S.activation(out=r_bc[:], in_=r_bc[:], func=AF.Exp,
                               scale=mhalf[:])
            s_rstd = sS.inc(ins)
            V.wait_ge(sS.h, s_rstd)
            return sV.v

        with ExitStack() as sAB:
            eAB = sAB.enter_context
            z = eAB(nc.sbuf_tensor("z", [P, DT * T], F32))

            sIN.inc(SP.dma_start(out=spar[:], in_=par), 16)
            par_ld = sIN.v

            sV.inc(V.memset(ones_bc[:], 1.0))
            sV.inc(V.memset(epsc[:], EPS))
            sV.inc(V.memset(mhalf[:], -0.5))

            with ExitStack() as sX:
                eX = sX.enter_context
                x = eX(nc.sbuf_tensor("x", [P, DT * T], F32))
                xbf = eX(nc.sbuf_tensor("xbf", [P, DT * T], BF16))

                xt_r = xt.rearrange("(dt p) t -> p dt t", p=P)
                sIN.inc(SP.dma_start(
                    out=x[:].rearrange("p (dt t) -> p dt t", t=T),
                    in_=xt_r), 16)
                x_ld = sIN.v
                sWD.inc(GP.dma_start(
                    out=xbf[:].rearrange("p (dt t) -> p dt t", t=T),
                    in_=xt_r), 16)
                xbf_ld = sWD.v

                V.wait_ge(sWD.h, xbf_ld)
                V.wait_ge(sIN.h, x_ld)
                for ft in range(T // FT):
                    emit_ln_stats_ft(
                        lambda dt, ft=ft: ("res", xbf[:, dt * T + ft * FT:
                                                      dt * T + ft * FT + FT]))
                    for dt in range(DT):
                        xap = x[:, dt * T + ft * FT: dt * T + ft * FT + FT]
                        zap = z[:, dt * T + ft * FT: dt * T + ft * FT + FT]
                        V.tensor_tensor(out=zap, in0=xap, in1=m_bc[:],
                                        op=ALU.subtract)
                        V.tensor_tensor(out=zap, in0=zap, in1=r_bc[:],
                                        op=ALU.mult)
                z_done = sV.inc(V.last())
            dbg_dump(SP, "z", z[:].rearrange("p (dt t) -> p dt t", t=T),
                     (sV, z_done))

            V_DTS = range(0, 6)
            P_DTS = range(6, 8)
            VA, VB = 6, 8
            with ExitStack() as sTab:
                eTb = sTab.enter_context
                ktab = eTb(nc.sbuf_tensor("ktab", [P, DT * 16], F32))
                ubf = eTb(nc.sbuf_tensor("ubf", [P, DT * 256], BF16))
                pvbf = eTb(nc.sbuf_tensor("pvbf", [P, DT * 256], BF16))
                mbf = eTb(nc.sbuf_tensor("mbf", [P, DT * 256], BF16))
                _tV = eTb(nc.sbuf_tensor("tmpV0", [P, 2048], F32))
                _tP = eTb(nc.sbuf_tensor("tmpP0", [P, 2048], F32))
                tmpV = [_tV, _tV]
                tmpP = [_tP, _tP]
                hcur = eTb(nc.sbuf_tensor("hcur", [P, 2 * 128], BF16))
                redv = eTb(nc.sbuf_tensor("redv", [P, 128], BF16))
                corr2 = eTb(nc.sbuf_tensor("corr2", [P, 4 * 256], BF16))
                htg = eTb(nc.sbuf_tensor("htg", [P, 4 * 256], BF16))
                zbf = eTb(nc.sbuf_tensor("zbf", [P, DT * TQ], BF16))

                kv = ktab[:].rearrange("p (dt j) -> p dt j", dt=DT)
                ubfv = ubf[:].rearrange("p (dt t n) -> p dt t n", dt=DT, t=16)
                pvbfv = pvbf[:].rearrange("p (dt n t) -> p dt n t", dt=DT,
                                          n=16)
                mbv = mbf[:].rearrange("p (dt n m) -> p dt n m", dt=DT, n=16)

                with ExitStack() as sT2:
                    eT2 = sT2.enter_context
                    sparA = eT2(nc.sbuf_tensor("sparA", [P, DT * 256], F32))
                    utab32 = eT2(nc.sbuf_tensor("utab32", [P, DT * 272], F32))
                    pvt32 = eT2(nc.sbuf_tensor("pvt32", [P, DT * 256], F32))
                    Abf = eT2(nc.sbuf_tensor("Abf", [P, DT * 256], BF16))
                    lvlA = eT2(nc.sbuf_tensor("lvlA", [P, DT * 256], BF16))
                    lvlB = eT2(nc.sbuf_tensor("lvlB", [P, DT * 256], BF16))
                    srcT = eT2(nc.sbuf_tensor("srcT", [P, DT * 256], BF16))

                    SP.wait_ge(sV.h, z_done)
                    sIN.inc(SP.dma_start(out=sparA[:], in_=parA), 16)
                    sparA_ld = sIN.v
                    V.wait_ge(sIN.h, sparA_ld)
                    S.wait_ge(sIN.h, sparA_ld)
                    S.wait_ge(sV.h, z_done)
                    S.low_precision[0] = True
                    ins = S.copy(out=Abf[:], in_=sparA[:])
                    S.low_precision[0] = False
                    abf_done = sS.inc(ins)

                    uv17 = utab32[:].rearrange("p (dt r n) -> p dt r n",
                                               dt=DT, r=17)
                    pvv = pvt32[:].rearrange("p (dt t n) -> p dt t n",
                                             dt=DT, t=16)
                    cmv = pcol("Cm", 0, 128).rearrange("p (dt n) -> p dt n",
                                                       dt=DT)
                    bmv = pcol("Bm", 0, 128).rearrange("p (dt n) -> p dt n",
                                                       dt=DT)
                    V.tensor_copy(out=uv17[:, :, 0, :], in_=cmv)
                    V.tensor_copy(out=pvv[:, :, 15, :], in_=bmv)
                    A_nm = sparA[:].rearrange("p (dt n m) -> p dt n m",
                                              dt=DT, n=16)
                    A_T = A_nm.transpose([0, 1, 3, 2])
                    tv0 = tmpV[0][:].rearrange("p (dt a b) -> p dt a b",
                                               dt=DT, a=16)
                    tv1 = tmpV[1][:].rearrange("p (dt a b) -> p dt a b",
                                               dt=DT, a=16)
                    for j in range(16):
                        V.tensor_tensor(
                            out=tv0,
                            in0=uv17[:, :, j, :].unsqueeze(2)
                            .broadcast_to([P, DT, 16, 16]),
                            in1=A_T, op=ALU.mult)
                        V.tensor_reduce(out=uv17[:, :, j + 1, :], in_=tv0,
                                        axis=AX.X, op=ALU.add)
                        if j < 15:
                            V.tensor_tensor(
                                out=tv1,
                                in0=pvv[:, :, 15 - j, :].unsqueeze(2)
                                .broadcast_to([P, DT, 16, 16]),
                                in1=A_nm, op=ALU.mult)
                            V.tensor_reduce(out=pvv[:, :, 14 - j, :],
                                            in_=tv1, axis=AX.X, op=ALU.add)
                    V.tensor_tensor(
                        out=tv0, in0=uv17[:, :, 0:16, :],
                        in1=bmv.unsqueeze(2).broadcast_to([P, DT, 16, 16]),
                        op=ALU.mult)
                    V.tensor_reduce(out=kv, in_=tv0, axis=AX.X, op=ALU.add)
                    V.tensor_tensor(out=kv[:, :, 0:1], in0=kv[:, :, 0:1],
                                    in1=pcol("Dm", 0, DT).unsqueeze(2),
                                    op=ALU.add)
                    V.low_precision[0] = True
                    V.tensor_copy(out=ubfv, in_=uv17[:, :, 1:17, :])
                    ins = V.tensor_copy(out=pvbfv,
                                        in_=pvv.transpose([0, 1, 3, 2]))
                    V.low_precision[0] = False
                    uvdone = sV.inc(ins)

                    GP.wait_ge(sV.h, z_done)
                    GP.wait_ge(sS.h, abf_done)
                    V.wait_ge(sS.h, abf_done)
                    chain = [Abf, lvlA, lvlB, lvlA, mbf]
                    v_done_lvl = 0
                    p_done_lvl = 0
                    for lvl in range(4):
                        sv = chain[lvl][:].rearrange(
                            "p (dt n m) -> p dt n m", dt=DT, n=16)
                        dv = chain[lvl + 1][:].rearrange(
                            "p (dt n m) -> p dt n m", dt=DT, n=16)
                        stv = srcT[:].rearrange("p (dt m k) -> p dt m k",
                                                dt=DT, m=16)
                        if lvl:
                            V.wait_ge(sGP.h, p_done_lvl)
                            GP.wait_ge(sV.h, v_done_lvl)
                        V.low_precision[0] = True
                        ins = V.tensor_copy(out=stv,
                                            in_=sv.transpose([0, 1, 3, 2]))
                        V.low_precision[0] = False
                        tcp = sV.inc(ins)
                        GP.wait_ge(sV.h, tcp)
                        for eng, T2, dts, lp in ((V, tmpV, V_DTS, V),
                                                 (GP, tmpP, P_DTS, GP)):
                            for i, dt in enumerate(dts):
                                tb = T2[i % 2][:].bitcast(BF16).rearrange(
                                    "p (n m k) -> p n m k", n=16, m=16)
                                lp.low_precision[0] = True
                                eng.tensor_tensor(
                                    out=tb,
                                    in0=sv[:, dt].unsqueeze(2)
                                    .broadcast_to([P, 16, 16, 16]),
                                    in1=stv[:, dt].unsqueeze(1)
                                    .broadcast_to([P, 16, 16, 16]),
                                    op=ALU.mult)
                                red16(eng, dv[:, dt], tb)
                                lp.low_precision[0] = False
                        v_done_lvl = sV.inc(V.last())
                        p_done_lvl = sGP.inc(GP.last())
                    for nm, tbuf in (("M", mbf), ("pv", pvbf)):
                        if nm in dbg:
                            SP.wait_ge(sV.h, v_done_lvl)
                            SP.wait_ge(sGP.h, p_done_lvl)
                            ins = SP.dma_start(out=dbg[nm], in_=tbuf[:])
                            done = sOUT.inc(ins, 16)
                            V.wait_ge(sOUT.h, done)

                Y = eTb(nc.sbuf_tensor("Y", [P, DT * TQ], F32))
                Hb = eTb(nc.sbuf_tensor("Hb", [P, DT * 16 * 33], BF16))
                g1 = eTb(nc.sbuf_tensor("g1", [P, DT * TQ], BF16))
                wo_ch = eTb(nc.sbuf_tensor("wo_ch", [P, 2 * DT * P], BF16))

                wo_cons = [0, 0]
                wo_wb = [0, 0]
                bank_rr = [0]

                def next_bank():
                    b = bank_rr[0]
                    bank_rr[0] = (b + 1) % 4
                    return b

                def hbv_of(dt):
                    return Hb[:, dt * 528: dt * 528 + 528].rearrange(
                        "p (n c) -> p n c", n=16)

                hball = Hb[:].rearrange("p (dt n c) -> p dt n c", dt=DT,
                                        n=16)
                hc = hcur[:].rearrange("p (pp d n) -> p pp d n", pp=2, d=DT)
                rv = redv[:].rearrange("p (d n) -> p d n", d=DT)
                GP.wait_ge(sV.h, uvdone)

                ppg = [0]
                prev_g1_all = [0]
                prev_innerV = [0]
                prev_hle_p = [0]
                prev_wout = [0]
                for q in range(NQ):
                    hb = q * TQ
                    if prev_hle_p[0]:
                        S.wait_ge(sGP.h, prev_hle_p[0])
                    S.low_precision[0] = True
                    ins = S.copy(
                        out=zbf[:].rearrange("p (dt t) -> p dt t", t=TQ),
                        in_=z[:].rearrange("p (dt t) -> p dt t", t=T)
                        [:, :, hb:hb + TQ])
                    S.low_precision[0] = False
                    zbf_done = sS.inc(ins)

                    if prev_g1_all[0]:
                        V.wait_ge(sS.h, prev_g1_all[0])
                        GP.wait_ge(sS.h, prev_g1_all[0])
                        GP.wait_ge(sV.h, prev_innerV[0])
                    for eng, dts in ((V, V_DTS), (GP, P_DTS)):
                        for dt in dts:
                            zf = z[:, dt * T + hb: dt * T + hb + TQ]
                            zv = zf.rearrange("p (c t) -> p c t", t=L)
                            yf = Y[:, dt * TQ: dt * TQ + TQ]
                            yv = yf.rearrange("p (c t) -> p c t", t=L)
                            if eng is V:
                                eng.tensor_scalar(
                                    out=yf, in0=zf,
                                    scalar1=ktab[:, dt * 16:dt * 16 + 1],
                                    scalar2=None, op0=ALU.mult)
                                pass
                                for j in range(1, L):
                                    eng.scalar_tensor_tensor(
                                        out=yv[:, :, j:L],
                                        in0=zv[:, :, 0:L - j],
                                        scalar=ktab[:, dt * 16 + j:
                                                    dt * 16 + j + 1],
                                        in1=yv[:, :, j:L], op0=ALU.mult,
                                        op1=ALU.add)
                            else:
                                eng.tensor_tensor(
                                    out=yf, in0=zf,
                                    in1=ktab[:, dt * 16:dt * 16 + 1]
                                    .broadcast_to([P, TQ]),
                                    op=ALU.mult)
                                tsc = tmpP[0][:, :CHQ * L]
                                tv = tsc.rearrange("p (c t) -> p c t", t=L)
                                for j in range(1, L):
                                    eng.tensor_tensor(
                                        out=tv[:, :, 0:L - j],
                                        in0=zv[:, :, 0:L - j],
                                        in1=ktab[:, dt * 16 + j:
                                                 dt * 16 + j + 1]
                                        .unsqueeze(2)
                                        .broadcast_to([P, CHQ, L - j]),
                                        op=ALU.mult)
                                    eng.tensor_tensor(
                                        out=yv[:, :, j:L],
                                        in0=yv[:, :, j:L],
                                        in1=tv[:, :, 0:L - j], op=ALU.add)

                    V.wait_ge(sS.h, zbf_done)
                    GP.wait_ge(sS.h, zbf_done)
                    zb = zbf[:].rearrange("p (dt c t) -> p dt c t", dt=DT,
                                          t=L)
                    for eng, T2, dts in ((V, tmpV, V_DTS), (GP, tmpP, P_DTS)):
                        for dt in dts:
                            for g in range(CHQ // 16):
                                tb = T2[g % 2][:].bitcast(BF16).rearrange(
                                    "p (n c t) -> p n c t", n=16, c=16)
                                eng.low_precision[0] = True
                                eng.tensor_tensor(
                                    out=tb,
                                    in0=zb[:, dt, 16 * g:16 * g + 16, :]
                                    .unsqueeze(1)
                                    .broadcast_to([P, 16, 16, L]),
                                    in1=pvbfv[:, dt].unsqueeze(2)
                                    .broadcast_to([P, 16, 16, L]),
                                    op=ALU.mult)
                                red16(eng, hball[:, dt, :,
                                                 1 + 16 * g: 17 + 16 * g],
                                      tb)
                                eng.low_precision[0] = False
                    if q == 0 and "Hle0" in dbg:
                        hle_v = sV.inc(V.last())
                        hle_p = sGP.inc(GP.last())
                        SP.wait_ge(sV.h, hle_v)
                        SP.wait_ge(sGP.h, hle_p)
                        ins = SP.dma_start(out=dbg["Hle0"], in_=Hb[:])
                        done = sOUT.inc(ins, 16)
                        V.wait_ge(sOUT.h, done)

                    c0 = 1 if q == 0 else 0
                    hle_p = sGP.inc(GP.last())
                    prev_hle_p[0] = hle_p
                    V.wait_ge(sGP.h, hle_p)
                    dsls = ((0, 3), (3, 6), (6, 8))
                    pp = ppg[0]
                    V.low_precision[0] = True
                    for a, b in dsls:
                        if q == 0:
                            V.tensor_copy(
                                out=hc[:, pp, a:b, :].unsqueeze(3),
                                in_=hball[:, a:b, :, 1:2])
                        else:
                            V.tensor_copy(
                                out=hball[:, a:b, :, 0:1],
                                in_=hc[:, pp, a:b, :].unsqueeze(3))
                    V.low_precision[0] = False
                    V.wait_ge(sV.h, sV.inc(V.last()))
                    for c in range(c0, CHQ):
                        V.low_precision[0] = True
                        for k2, (a, b) in enumerate(dsls):
                            nd = b - a
                            tb = tmpV[k2 % 2][:].bitcast(BF16) \
                                [:, :nd * 256].rearrange(
                                    "p (d n m) -> p d n m", d=nd, n=16)
                            V.tensor_tensor(
                                out=tb, in0=mbv[:, a:b],
                                in1=hc[:, pp, a:b, :].unsqueeze(2)
                                .broadcast_to([P, nd, 16, 16]),
                                op=ALU.mult)
                            red16(V, rv[:, a:b, :], tb)
                        for a, b in dsls:
                            V.tensor_tensor(
                                out=hc[:, 1 - pp, a:b, :].unsqueeze(3),
                                in0=rv[:, a:b, :].unsqueeze(3),
                                in1=hball[:, a:b, :, c + 1:c + 2],
                                op=ALU.add)
                        for a, b in dsls:
                            V.tensor_tensor(
                                out=hball[:, a:b, :, c + 1:c + 2],
                                in0=hball[:, a:b, :, c + 1:c + 2],
                                in1=rv[:, a:b, :].unsqueeze(3),
                                op=ALU.add)
                        V.low_precision[0] = False
                        pp = 1 - pp
                    ppg[0] = pp
                    bnd_v = sV.inc(V.last())
                    GP.wait_ge(sV.h, bnd_v)

                    for ei, (eng, T2, dts, sE) in enumerate(
                            ((V, tmpV, V_DTS, sV), (GP, tmpP, P_DTS, sGP))):
                        for dt in dts:
                            yv = Y[:, dt * TQ: dt * TQ + TQ].rearrange(
                                "p (c t) -> p c t", t=L)
                            gis = list(enumerate(range(c0, CHQ, 16)))
                            eng.low_precision[0] = True
                            for gi, cl in gis:
                                w = min(16, CHQ - cl)
                                sl = (2 * ei + gi % 2) * 256
                                ht = htg[:, sl:sl + w * 16].rearrange(
                                    "p (c n) -> p c n", c=w)
                                eng.tensor_copy(
                                    out=ht,
                                    in_=hball[:, dt, :, cl:cl + w]
                                    .transpose([0, 2, 1]))
                            eng.low_precision[0] = False
                            eng.wait_ge(sE.h, sE.inc(eng.last()))
                            eng.low_precision[0] = True
                            for gi, cl in gis:
                                w = min(16, CHQ - cl)
                                sl = (2 * ei + gi % 2) * 256
                                ht = htg[:, sl:sl + w * 16].rearrange(
                                    "p (c n) -> p c n", c=w)
                                tb = T2[gi % 2][:].bitcast(BF16) \
                                    [:, :16 * w * 16].rearrange(
                                        "p (t c n) -> p t c n", t=16, c=w)
                                eng.tensor_tensor(
                                    out=tb,
                                    in0=ubfv[:, dt].unsqueeze(2)
                                    .broadcast_to([P, 16, w, 16]),
                                    in1=ht.unsqueeze(1)
                                    .broadcast_to([P, 16, w, 16]),
                                    op=ALU.mult)
                                cb = corr2[:, sl:sl + 16 * w].rearrange(
                                    "p (t c) -> p t c", t=16)
                                red16(eng, cb, tb)
                            eng.low_precision[0] = False
                            eng.wait_ge(sE.h, sE.inc(eng.last()))
                            for gi, cl in gis:
                                w = min(16, CHQ - cl)
                                sl = (2 * ei + gi % 2) * 256
                                cb = corr2[:, sl:sl + 16 * w].rearrange(
                                    "p (t c) -> p t c", t=16)
                                eng.tensor_tensor(
                                    out=yv[:, cl:cl + w, :],
                                    in0=yv[:, cl:cl + w, :],
                                    in1=cb.transpose([0, 2, 1]), op=ALU.add)
                    pool_half = sGP.inc(GP.last())
                    V.wait_ge(sGP.h, pool_half)
                    S.wait_ge(sGP.h, pool_half)
                    y_done = sV.inc(V.last())
                    if q == 0 and "Hb0" in dbg:
                        SP.wait_ge(sV.h, y_done)
                        ins = SP.dma_start(out=dbg["Hb0"], in_=Hb[:])
                        done = sOUT.inc(ins, 16)
                        V.wait_ge(sOUT.h, done)
                    dbg_dump(SP, f"Y{q}",
                             Y[:].rearrange("p (dt t) -> p dt t", t=TQ),
                             (sV, y_done))

                    S.wait_ge(sV.h, y_done)
                    if prev_wout[0]:
                        S.wait_ge(sT.h, prev_wout[0])
                    g1_done = {}
                    for ft in range(TQ // FT):
                        emit_ln_stats_ft(
                            lambda dt, ft=ft: ("copy",
                                               Y[:, dt * TQ + ft * FT:
                                                 dt * TQ + ft * FT + FT]))
                        for dt in range(DT):
                            yap = Y[:, dt * TQ + ft * FT:
                                    dt * TQ + ft * FT + FT]
                            tsc = acts[:, (dt % 2) * FT:(dt % 2) * FT + FT]
                            if (ft, dt) >= (0, 2):
                                key = (ft, dt - 2) if dt >= 2 else \
                                    (ft - 1, DT - 2 + dt)
                                V.wait_ge(sS.h, g1_done[key])
                            V.tensor_tensor(out=tsc, in0=yap, in1=m_bc[:],
                                            op=ALU.subtract)
                            ins = V.tensor_tensor(out=tsc, in0=tsc,
                                                  in1=r_bc[:], op=ALU.mult)
                            vv = sV.inc(ins)
                            S.wait_ge(sV.h, vv)
                            ins = S.activation(
                                out=g1[:, dt * TQ + ft * FT:
                                       dt * TQ + ft * FT + FT],
                                in_=tsc, func=AF.Gelu,
                                scale=pcol("lnw", dt, 1),
                                bias=pcol("lnb", dt, 1))
                            g1_done[(ft, dt)] = sS.inc(ins)
                    g1_all = g1_done[(TQ // FT - 1, DT - 1)]
                    prev_g1_all[0] = g1_all
                    prev_innerV[0] = sV.v
                    if q == 0:
                        dbg_dump(SP, "g10",
                                 g1[:].rearrange("p (dt t) -> p dt t", t=TQ),
                                 (sS, g1_all))

                    for ob in range(DT):
                        slot = ob % 2
                        chf = wo_ch[:, slot * DT * P: slot * DT * P + DT * P]
                        ch = chf.rearrange("p (it o) -> p it o", it=DT)
                        if wo_cons[slot]:
                            GP.wait_ge(sT.h, wo_cons[slot])
                        if wo_wb[slot]:
                            GP.wait_ge(sWB.h, wo_wb[slot])
                        if q == 0:
                            sWD.inc(GP.dma_start(
                                out=ch, in_=wout_t[:, ob * P:ob * P + P]
                                .rearrange("(it p) o -> p it o", p=P)), 16)
                            ch_ld = sWD.v
                            SP.wait_ge(sWD.h, ch_ld)
                            sWB.inc(SP.dma_start(
                                out=woutb[ob * P:(ob + 1) * P, :],
                                in_=chf), 16)
                            wo_wb[slot] = sWB.v
                        else:
                            sWD.inc(GP.dma_start(
                                out=chf,
                                in_=woutb[ob * P:(ob + 1) * P, :]), 16)
                            ch_ld = sWD.v
                        for ftl in range(TQ // FT):
                            bank = next_bank()
                            TT.wait_ge(sWD.h, ch_ld)
                            TT.wait_ge(sS.h, g1_all)
                            acquire_bank(bank)
                            for it in range(DT):
                                ins = TT.matmul(
                                    out=pb[bank][:], lhsT=ch[:, it, :],
                                    rhs=g1[:, it * TQ + ftl * FT:
                                           it * TQ + ftl * FT + FT],
                                    start=(it == 0), stop=(it == DT - 1))
                            tmm = sT.inc(ins)
                            V.wait_ge(sT.h, tmm)
                            zap = z[:, ob * T + hb + ftl * FT:
                                    ob * T + hb + ftl * FT + FT]
                            ins = V.scalar_tensor_tensor(
                                out=zap, in0=pb[bank][:],
                                scalar=pcol("bout", ob, 1), in1=zap,
                                op0=ALU.add, op1=ALU.add)
                            bank_free[bank] = (sV, sV.inc(ins))
                        wo_cons[slot] = sT.v
                    prev_wout[0] = sT.v
                a_done = sV.v
            dbg_dump(SP, "a", z[:].rearrange("p (dt t) -> p dt t", t=T),
                     (sV, a_done))

            with ExitStack() as sXX:
                eXX = sXX.enter_context
                x0s = eXX(nc.sbuf_tensor("x0s", [P, DT * TQ], F32))
                x1s = eXX(nc.sbuf_tensor("x1s", [P, DT * TQ], F32))
                for th in range(NQ):
                    if th >= 1:
                        SP.wait_ge(sV.h, sV.v)
                    SP.wait_ge(sV.h, a_done)
                    sIN.inc(SP.dma_start(
                        out=x0s[:].rearrange("p (dt t) -> p dt t", t=TQ),
                        in_=xt.rearrange("(dt p) t -> p dt t", p=P)
                        [:, :, th * TQ:th * TQ + TQ]), 16)
                    x0_ld = sIN.v
                    S.wait_ge(sV.h, a_done)
                    for ft in range(TQ // FT):
                        emit_ln_stats_ft(
                            lambda dt, ft=ft, th=th:
                            ("copy", z[:, dt * T + th * TQ + ft * FT:
                                       dt * T + th * TQ + ft * FT + FT]))
                        V.wait_ge(sIN.h, x0_ld)
                        for dt in range(DT):
                            aap = z[:, dt * T + th * TQ + ft * FT:
                                    dt * T + th * TQ + ft * FT + FT]
                            tsc = acts[:, (dt % 2) * FT:(dt % 2) * FT + FT]
                            x0ap = x0s[:, dt * TQ + ft * FT:
                                       dt * TQ + ft * FT + FT]
                            x1ap = x1s[:, dt * TQ + ft * FT:
                                       dt * TQ + ft * FT + FT]
                            V.tensor_tensor(out=tsc, in0=aap, in1=m_bc[:],
                                            op=ALU.subtract)
                            V.tensor_tensor(out=tsc, in0=tsc, in1=r_bc[:],
                                            op=ALU.mult)
                            V.tensor_tensor(out=x1ap, in0=tsc, in1=x0ap,
                                            op=ALU.add)
                    x1_done = sV.inc(V.last())
                    SP.wait_ge(sV.h, x1_done)
                    sOUT.inc(SP.dma_start(
                        out=x1sp.rearrange("(dt p) t -> p dt t", p=P)
                        [:, :, th * TQ:th * TQ + TQ],
                        in_=x1s[:].rearrange("p (dt t) -> p dt t", t=TQ)), 16)
        x1_spilled = sOUT.v
        if "x1" in dbg:
            SP.wait_ge(sOUT.h, x1_spilled)
            sOUT.inc(SP.dma_start(out=dbg["x1"], in_=x1sp), 16)
            x1_spilled = sOUT.v

        with ExitStack() as sFF:
            eF = sFF.enter_context
            x1t = eF(nc.sbuf_tensor("x1t", [P, DT * TQ], F32))
            hbuf = eF(nc.sbuf_tensor("hbuf", [P, OT4 * TQ], BF16))
            ch1 = eF(nc.sbuf_tensor("ch1", [P, 2 * DT * P], BF16))
            ch2 = eF(nc.sbuf_tensor("ch2", [P, 2 * DT * P], BF16))
            CH3W = OT4 * P
            ch3 = eF(nc.sbuf_tensor("ch3", [P, 2 * CH3W], BF16))

            ch_cons = {"w12": [0, 0], "c3": [0, 0]}
            wb_done = {"w12": [0, 0], "c3": [0, 0]}
            tmpa_guard = [0, 0]
            bank_rr = [0]
            GP.wait_ge(sV.h, sV.v)
            GP.wait_ge(sS.h, sS.v)
            SP.wait_ge(sV.h, sV.v)
            SP.wait_ge(sS.h, sS.v)

            def next_bank():
                b = bank_rr[0]
                bank_rr[0] = (b + 1) % 4
                return b

            def chunk_load(flat_ap, cube_ap, wsrc, wsrcb, ob, key, slot,
                           do_wb):
                if ch_cons[key][slot]:
                    GP.wait_ge(sT.h, ch_cons[key][slot])
                if wb_done[key][slot]:
                    GP.wait_ge(sWB.h, wb_done[key][slot])
                if do_wb:
                    sWD.inc(GP.dma_start(
                        out=cube_ap, in_=wsrc[:, ob * P:ob * P + P]
                        .rearrange("(it p) o -> p it o", p=P)), 16)
                    SP.wait_ge(sWD.h, sWD.v)
                    sWB.inc(SP.dma_start(
                        out=wsrcb[ob * P:(ob + 1) * P, :], in_=flat_ap), 16)
                    wb_done[key][slot] = sWB.v
                else:
                    sWD.inc(GP.dma_start(
                        out=flat_ap, in_=wsrcb[ob * P:(ob + 1) * P, :]), 16)
                return sWD.v

            x1t_reader_v = 0
            for fq in range(NQ):
                wb0 = (fq == 0)
                SP.wait_ge(sOUT.h, x1_spilled)
                if x1t_reader_v:
                    SP.wait_ge(sV.h, x1t_reader_v)
                sIN.inc(SP.dma_start(
                    out=x1t[:].rearrange("p (dt t) -> p dt t", t=TQ),
                    in_=x1sp.rearrange("(dt p) t -> p dt t", p=P)
                    [:, :, fq * TQ:fq * TQ + TQ]), 16)
                x1_ld = sIN.v

                with ExitStack() as sZ2:
                    z2t = sZ2.enter_context(
                        nc.sbuf_tensor(f"z2t{fq}", [P, DT * TQ], BF16))
                    S.wait_ge(sIN.h, x1_ld)
                    if fq:
                        V.wait_ge(sOUT.h, sOUT.v)
                        S.wait_ge(sOUT.h, sOUT.v)
                    for ft in range(TQ // FT):
                        emit_ln_stats_ft(
                            lambda dt, ft=ft: ("copy",
                                               x1t[:, dt * TQ + ft * FT:
                                                   dt * TQ + ft * FT + FT]))
                        for dt in range(DT):
                            xap = x1t[:, dt * TQ + ft * FT:
                                      dt * TQ + ft * FT + FT]
                            tsc = acts[:, (dt % 2) * FT:(dt % 2) * FT + FT]
                            zap = z2t[:, dt * TQ + ft * FT:
                                      dt * TQ + ft * FT + FT]
                            V.tensor_tensor(out=tsc, in0=xap, in1=m_bc[:],
                                            op=ALU.subtract)
                            V.tensor_tensor(out=zap, in0=tsc, in1=r_bc[:],
                                            op=ALU.mult)
                    z2_done = sV.inc(V.last())
                    if fq == 0:
                        dbg_dump(SP, "z20",
                                 z2t[:].rearrange("p (dt t) -> p dt t", t=TQ),
                                 (sV, z2_done))

                    for ob in range(OT4):
                        slot = ob % 2
                        c1f = ch1[:, slot * DT * P:(slot + 1) * DT * P]
                        c2f = ch2[:, slot * DT * P:(slot + 1) * DT * P]
                        c1 = c1f.rearrange("p (it o) -> p it o", it=DT)
                        c2 = c2f.rearrange("p (it o) -> p it o", it=DT)
                        chunk_load(c1f, c1, w1_t, w1b, ob, "w12", slot, wb0)
                        ch_ld = chunk_load(c2f, c2, w2_t, w2b, ob, "w12",
                                           slot, wb0)
                        for ftl in range(TQ // FT):
                            b1 = next_bank()
                            b2 = next_bank()
                            TT.wait_ge(sWD.h, ch_ld)
                            TT.wait_ge(sV.h, z2_done)
                            acquire_bank(b1)
                            acquire_bank(b2)
                            for it in range(DT):
                                TT.matmul(out=pb[b1][:], lhsT=c1[:, it, :],
                                          rhs=z2t[:, it * TQ + ftl * FT:
                                                  it * TQ + ftl * FT + FT],
                                          start=(it == 0),
                                          stop=(it == DT - 1))
                            for it in range(DT):
                                ins = TT.matmul(
                                    out=pb[b2][:], lhsT=c2[:, it, :],
                                    rhs=z2t[:, it * TQ + ftl * FT:
                                            it * TQ + ftl * FT + FT],
                                    start=(it == 0), stop=(it == DT - 1))
                            tmm = sT.inc(ins)
                            S.wait_ge(sT.h, tmm)
                            if tmpa_guard[ftl % 2]:
                                S.wait_ge(sV.h, tmpa_guard[ftl % 2])
                            tmpa = acts[:, (ftl % 2) * FT:(ftl % 2) * FT + FT]
                            ins = S.activation(out=tmpa, in_=pb[b1][:],
                                               func=AF.Silu,
                                               scale=pcol("s1", ob, 1))
                            sse = sS.inc(ins)
                            bank_free[b1] = (sS, sse)
                            V.wait_ge(sS.h, sse)
                            hap = hbuf[:, ob * TQ + ftl * FT:
                                       ob * TQ + ftl * FT + FT]
                            ins = V.scalar_tensor_tensor(
                                out=hap, in0=pb[b2][:],
                                scalar=pcol("s2", ob, 1), in1=tmpa,
                                op0=ALU.mult, op1=ALU.mult)
                            vval = sV.inc(ins)
                            bank_free[b2] = (sV, vval)
                            tmpa_guard[ftl % 2] = vval
                        ch_cons["w12"][slot] = sT.v
                    h_done = sV.v
                if fq == 0:
                    dbg_dump(SP, "h0",
                             hbuf[:].rearrange("p (ot t) -> p ot t", t=TQ),
                             (sV, h_done))

                with ExitStack() as sG:
                    gbuf = sG.enter_context(
                        nc.sbuf_tensor(f"gbuf{fq}", [P, OT4 * TQ], BF16))

                    def big_mm_phase(wsrc, wsrcb, rhs_buf, n_ob, scale_name,
                                     dst_fn, rhs_ready, act_func):
                        for ob in range(n_ob):
                            slot = ob % 2
                            ccf = ch3[:, slot * CH3W: slot * CH3W + CH3W]
                            cc = ccf.rearrange("p (it o) -> p it o", it=OT4)
                            ch_ld = chunk_load(ccf, cc, wsrc, wsrcb, ob,
                                               "c3", slot, wb0)
                            for ftl in range(TQ // FT):
                                bank = next_bank()
                                TT.wait_ge(sWD.h, ch_ld)
                                TT.wait_ge(rhs_ready[0].h, rhs_ready[1])
                                acquire_bank(bank)
                                for it in range(OT4):
                                    ins = TT.matmul(
                                        out=pb[bank][:], lhsT=cc[:, it, :],
                                        rhs=rhs_buf[:, it * TQ + ftl * FT:
                                                    it * TQ + ftl * FT + FT],
                                        start=(it == 0),
                                        stop=(it == OT4 - 1))
                                tmm = sT.inc(ins)
                                S.wait_ge(sT.h, tmm)
                                ins = S.activation(
                                    out=dst_fn(ob, ftl), in_=pb[bank][:],
                                    func=act_func,
                                    scale=pcol(scale_name, ob, 1))
                                bank_free[bank] = (sS, sS.inc(ins))
                            ch_cons["c3"][slot] = sT.v
                        return sS.v

                    g_done = big_mm_phase(
                        h1_t, h1b, hbuf, OT4, "s1h",
                        lambda ob, ftl: gbuf[:, ob * TQ + ftl * FT:
                                             ob * TQ + ftl * FT + FT],
                        (sV, h_done), AF.Gelu)
                    if fq == 0:
                        dbg_dump(SP, "g0",
                                 gbuf[:].rearrange("p (ot t) -> p ot t",
                                                   t=TQ),
                                 (sS, g_done))
                    k_done = big_mm_phase(
                        h2_t, h2b, gbuf, OT4, "s2h",
                        lambda ob, ftl: hbuf[:, ob * TQ + ftl * FT:
                                             ob * TQ + ftl * FT + FT],
                        (sS, g_done), AF.Gelu)
                    fbuf = gbuf[:].bitcast(F32)
                    f_done = big_mm_phase(
                        w3_t, w3b, hbuf, DT, "s3",
                        lambda ob, ftl: fbuf[:, ob * TQ + ftl * FT:
                                             ob * TQ + ftl * FT + FT],
                        (sS, k_done), AF.Copy)
                    if fq == 0:
                        dbg_dump(SP, "f0",
                                 fbuf[:, 0:DT * TQ].rearrange(
                                     "p (dt t) -> p dt t", t=TQ),
                                 (sS, f_done))

                    S.wait_ge(sS.h, f_done)
                    for ft in range(TQ // FT):
                        emit_ln_stats_ft(
                            lambda dt, ft=ft: ("copy",
                                               fbuf[:, dt * TQ + ft * FT:
                                                    dt * TQ + ft * FT + FT]))
                        for dt in range(DT):
                            fap = fbuf[:, dt * TQ + ft * FT:
                                       dt * TQ + ft * FT + FT]
                            tsc = acts[:, (dt % 2) * FT:(dt % 2) * FT + FT]
                            x1ap = x1t[:, dt * TQ + ft * FT:
                                       dt * TQ + ft * FT + FT]
                            V.tensor_tensor(out=tsc, in0=fap, in1=m_bc[:],
                                            op=ALU.subtract)
                            V.tensor_tensor(out=tsc, in0=tsc, in1=r_bc[:],
                                            op=ALU.mult)
                            ins = V.tensor_tensor(out=fap, in0=tsc, in1=x1ap,
                                                  op=ALU.add)
                            vdone = sV.inc(ins)
                            SP.wait_ge(sV.h, vdone)
                            sOUT.inc(SP.dma_start(
                                out=y_out.rearrange("(dt p) t -> p dt t",
                                                    p=P)
                                [:, dt, fq * TQ + ft * FT:
                                 fq * TQ + ft * FT + FT],
                                in_=fap), 16)
                    x1t_reader_v = sV.v
            SP.wait_ge(sOUT.h, sOUT.v)

        with nc.Block() as block:
            @block.sync
            def _(e):
                _replay(nc, SP, e)

            @block.gpsimd
            def _(e):
                _replay(nc, GP, e)

            @block.vector
            def _(e):
                _replay(nc, V, e)

            @block.scalar
            def _(e):
                _replay(nc, S, e)

            @block.tensor
            def _(e):
                _replay(nc, TT, e)
    return nc


_HOST_CACHE = {}


def _prep_shared(inp):
    def t(a):
        return np.ascontiguousarray(a.T)

    A = inp["L0_attn_A"].reshape(D, 256)
    parA = np.ascontiguousarray(
        A.reshape(DT, P, 256).transpose(1, 0, 2).reshape(P, DT * 256))

    def cols(a, w):
        return a.reshape(w, P).T

    par = np.concatenate([
        inp["L0_attn_Bm"].reshape(DT, P, 16).transpose(1, 0, 2).reshape(P, 128),
        inp["L0_attn_Cm"].reshape(DT, P, 16).transpose(1, 0, 2).reshape(P, 128),
        cols(inp["L0_attn_Dm"], DT),
        cols(inp["L0_attn_ln_w"], DT),
        cols(inp["L0_attn_ln_b"], DT),
        cols(inp["L0_attn_b_out"], DT),
        cols(inp["L0_ffn_w1_s"], OT4),
        cols(inp["L0_ffn_w2_s"], OT4),
        cols(inp["L0_ffn_h1_s"], OT4),
        cols(inp["L0_ffn_h2_s"], OT4),
        cols(inp["L0_ffn_w3_s"], DT),
    ], axis=1)
    par = np.ascontiguousarray(par.astype(np.float32))
    assert par.shape == (P, NPAR)

    return {
        "parA": parA.astype(np.float32),
        "par": par,
        "wout_t": t(inp["L0_attn_W_out"]),
        "w1_t": t(inp["L0_ffn_w1_w"]),
        "w2_t": t(inp["L0_ffn_w2_w"]),
        "h1_t": t(inp["L0_ffn_h1_w"]),
        "h2_t": t(inp["L0_ffn_h2_w"]),
        "w3_t": t(inp["L0_ffn_w3_w"]),
    }


def kernel(**inputs):
    x = inputs["L0_attn_x"]
    B = x.shape[0]
    shared = _prep_shared(inputs)
    nc = build_nc()
    in_maps = []
    for b in range(B):
        m = dict(shared)
        m["xt"] = np.ascontiguousarray(x[b].T)
        in_maps.append(m)
    res = run_bass_kernel_spmd(nc, in_maps, core_ids=list(range(B)))
    out = np.stack([res.results[b]["y_out"].T for b in range(B)], axis=0)
    return np.ascontiguousarray(out.astype(np.float32))


# version 15 (final)
# speedup vs baseline: 1.1421x; 1.1421x over previous
import numpy as np
from contextlib import ExitStack

import concourse.bass as bass
import concourse.mybir as mybir
from concourse.bass_utils import run_bass_kernel_spmd

F32 = mybir.dt.float32
BF16 = mybir.dt.bfloat16
AF = mybir.ActivationFunctionType
ALU = mybir.AluOpType
AX = mybir.AxisListType

P = 128
D = 1024
DT = D // P
T = 2048
TH = T // 2
TQ = 512
NQ = T // TQ
O4 = 4096
OT4 = O4 // P
L = 16
CH = TH // L
CHQ = TQ // L
FT = 512
EPS = 1e-5

_PC = {}
_off = 0
for _n, _w in [("Bm", 128), ("Cm", 128), ("Dm", DT), ("lnw", DT), ("lnb", DT),
               ("bout", DT), ("s1", OT4), ("s2", OT4),
               ("s1h", OT4), ("s2h", OT4), ("s3", DT)]:
    _PC[_n] = _off
    _off += _w
NPAR = _off

_DEBUG = []


class Ctr:
    def __init__(self, stack, nc, name):
        self.h = stack.enter_context(nc.semaphore(name))
        self.v = 0

    def inc(self, instr, n=1):
        instr.then_inc(self.h, n)
        self.v += n
        return self.v


class _RecInstr:
    __slots__ = ("incs",)

    def __init__(self):
        self.incs = []

    def then_inc(self, sem, n):
        self.incs.append((sem, n))
        return self


class _Rec:

    def __init__(self):
        self.ops = []
        self.low_precision = [False]

    def last(self):
        return self.ops[-1][3]

    def __getattr__(self, meth):
        if meth.startswith("_"):
            raise AttributeError(meth)
        ops = self.ops
        lp = self.low_precision

        def call(*a, **k):
            proxy = _RecInstr()
            ops.append((meth, a, k, proxy, lp[0]))
            return proxy
        return call


def _replay(nc, rec, eng):
    for meth, a, k, proxy, lp in rec.ops:
        if lp:
            with nc.allow_low_precision(reason="bf16 chunk-state storage"):
                ins = getattr(eng, meth)(*a, **k)
        else:
            ins = getattr(eng, meth)(*a, **k)
        for sem, n in proxy.incs:
            ins.then_inc(sem, n)


def build_nc():
    nc = bass.Bass("TRN2", target_bir_lowering=False, debug=False)

    xt = nc.dram_tensor("xt", [D, T], F32, kind="ExternalInput").ap()
    parA = nc.dram_tensor("parA", [P, DT * 256], F32, kind="ExternalInput").ap()
    par = nc.dram_tensor("par", [P, NPAR], F32, kind="ExternalInput").ap()
    wout_t = nc.dram_tensor("wout_t", [D, D], F32, kind="ExternalInput").ap()
    w1_t = nc.dram_tensor("w1_t", [D, O4], F32, kind="ExternalInput").ap()
    w2_t = nc.dram_tensor("w2_t", [D, O4], F32, kind="ExternalInput").ap()
    h1_t = nc.dram_tensor("h1_t", [O4, O4], F32, kind="ExternalInput").ap()
    h2_t = nc.dram_tensor("h2_t", [O4, O4], F32, kind="ExternalInput").ap()
    w3_t = nc.dram_tensor("w3_t", [O4, D], F32, kind="ExternalInput").ap()

    y_out = nc.dram_tensor("y_out", [D, T], F32, kind="ExternalOutput").ap()
    x1sp = nc.dram_tensor("x1sp", [D, T], F32, kind="ExternalOutput").ap()
    woutb = nc.dram_tensor("woutb", [DT * P, DT * P], BF16,
                           kind="Internal").ap()
    w1b = nc.dram_tensor("w1b", [OT4 * P, DT * P], BF16, kind="Internal").ap()
    w2b = nc.dram_tensor("w2b", [OT4 * P, DT * P], BF16, kind="Internal").ap()
    h1b = nc.dram_tensor("h1b", [OT4 * P, OT4 * P], BF16,
                         kind="Internal").ap()
    h2b = nc.dram_tensor("h2b", [OT4 * P, OT4 * P], BF16,
                         kind="Internal").ap()
    w3b = nc.dram_tensor("w3b", [DT * P, OT4 * P], BF16, kind="Internal").ap()

    dbg = {}
    for name in _DEBUG:
        shape = {"z": [D, T], "a": [D, T], "x1": [D, T],
                 "Y0": [D, TQ], "Y1": [D, TQ], "g10": [D, TQ],
                 "u": [P, DT * 272], "k": [P, DT * 16], "M": [P, DT * 256],
                 "pv": [P, DT * 256],
                 "h0": [O4, TQ], "g0": [O4, TQ], "f0": [D, TQ],
                 "z20": [D, TQ], "Hb0": [P, DT * 528],
                 "Hle0": [P, DT * 528]}[name]
        dt_ = BF16 if name in ("g10", "h0", "g0", "z20", "Hb0",
                               "Hle0", "M", "pv") else F32
        dbg[name] = nc.dram_tensor("dbg_" + name, shape, dt_,
                                   kind="ExternalOutput").ap()

    dbg_env = {}

    def dbg_dump(SP, name, sbuf_ap_3d, ctr_wait):
        if name not in dbg:
            return
        c, v = ctr_wait
        sOUT, V = dbg_env["sOUT"], dbg_env["V"]
        SP.wait_ge(c.h, v)
        ins = SP.dma_start(out=dbg[name].rearrange("(dt p) t -> p dt t", p=P),
                           in_=sbuf_ap_3d)
        done = sOUT.inc(ins, 16)
        V.wait_ge(sOUT.h, done)

    with ExitStack() as top:
        et = top.enter_context
        spar = et(nc.sbuf_tensor("spar", [P, NPAR], F32))
        ones_bc = et(nc.sbuf_tensor("ones_bc", [P, P], BF16))
        epsc = et(nc.sbuf_tensor("epsc", [P, 1], F32))
        mhalf = et(nc.sbuf_tensor("mhalf", [P, 1], F32))
        m_bc = et(nc.sbuf_tensor("m_bc", [P, FT], F32))
        r_bc = et(nc.sbuf_tensor("r_bc", [P, FT], F32))
        bfs = et(nc.sbuf_tensor("bfs", [P, 2 * FT], BF16))
        sqs = et(nc.sbuf_tensor("sqs", [P, 2 * FT], BF16))
        acts = et(nc.sbuf_tensor("acts", [P, 2 * FT], F32))

        pb = [et(nc.psum_tensor(f"pb{i}", [P, FT], F32)) for i in range(4)]
        p_m = et(nc.psum_tensor("p_m", [P, FT], F32))
        p_r = et(nc.psum_tensor("p_r", [P, FT], F32))

        sIN = Ctr(top, nc, "sIN")
        sWD = Ctr(top, nc, "sWD")
        sWB = Ctr(top, nc, "sWB")
        sGP = Ctr(top, nc, "sGP")
        sV = Ctr(top, nc, "sV")
        sS = Ctr(top, nc, "sS")
        sT = Ctr(top, nc, "sT")
        sOUT = Ctr(top, nc, "sOUT")

        SP, GP, V, S, TT = _Rec(), _Rec(), _Rec(), _Rec(), _Rec()
        dbg_env["sOUT"] = sOUT
        dbg_env["V"] = V

        bank_free = [None] * 4
        stats_free = [None]

        def acquire_bank(i):
            if bank_free[i] is not None:
                c, val = bank_free[i]
                TT.wait_ge(c.h, val)

        def pcol(name, i=0, w=1):
            return spar[:, _PC[name] + i:_PC[name] + i + w]

        def red16(eng, out3, tb4):
            if eng.ops is V.ops:
                eng.tensor_reduce(out=out3, in_=tb4, axis=AX.X, op=ALU.add)
                return
            for w2 in (8, 4, 2):
                eng.tensor_tensor(out=tb4[:, :, :, 0:w2],
                                  in0=tb4[:, :, :, 0:w2],
                                  in1=tb4[:, :, :, w2:2 * w2], op=ALU.add)
            eng.tensor_tensor(out=out3.unsqueeze(3), in0=tb4[:, :, :, 0:1],
                              in1=tb4[:, :, :, 1:2], op=ALU.add)

        prev_te_last = [0]

        def emit_ln_stats_ft(src_bf):
            te_done = {}
            if stats_free[0] is not None:
                c, val = stats_free[0]
                TT.wait_ge(c.h, val)
            for dt in range(DT):
                kind, ap = src_bf(dt)
                guard = te_done.get(dt - 2, prev_te_last[0] if dt < 2 else 0)
                if kind == "copy":
                    dst = bfs[:, (dt % 2) * FT:(dt % 2) * FT + FT]
                    if guard:
                        S.wait_ge(sT.h, guard)
                    ins = S.copy(out=dst, in_=ap)
                    s_cp = sS.inc(ins)
                    bf_ap = dst
                    V.wait_ge(sS.h, s_cp)
                else:
                    bf_ap = ap
                sq = sqs[:, (dt % 2) * FT:(dt % 2) * FT + FT]
                if guard:
                    V.wait_ge(sT.h, guard)
                ins = V.tensor_tensor(out=sq, in0=bf_ap, in1=bf_ap,
                                      op=ALU.mult)
                v_sq = sV.inc(ins)
                TT.wait_ge(sV.h, v_sq)
                TT.matmul(out=p_m[:], lhsT=ones_bc[:], rhs=bf_ap,
                          start=(dt == 0), stop=(dt == DT - 1))
                ins = TT.matmul(out=p_r[:], lhsT=ones_bc[:], rhs=sq,
                                start=(dt == 0), stop=(dt == DT - 1))
                te_done[dt] = sT.inc(ins)
            prev_te_last[0] = te_done[DT - 1]
            V.wait_ge(sT.h, te_done[DT - 1])
            V.tensor_scalar(out=m_bc[:], in0=p_m[:], scalar1=1.0 / D,
                            scalar2=None, op0=ALU.mult)
            ins = V.tensor_scalar(out=r_bc[:], in0=p_r[:], scalar1=1.0 / D,
                                  scalar2=None, op0=ALU.mult)
            stats_free[0] = (sV, sV.inc(ins))
            tmpf = sqs[:].bitcast(F32)[:, 0:FT]
            V.tensor_tensor(out=tmpf, in0=m_bc[:], in1=m_bc[:], op=ALU.mult)
            ins = V.tensor_tensor(out=r_bc[:], in0=r_bc[:], in1=tmpf,
                                  op=ALU.subtract)
            v_var = sV.inc(ins)
            S.wait_ge(sV.h, v_var)
            S.activation(out=r_bc[:], in_=r_bc[:], func=AF.Ln, bias=epsc[:])
            ins = S.activation(out=r_bc[:], in_=r_bc[:], func=AF.Exp,
                               scale=mhalf[:])
            s_rstd = sS.inc(ins)
            V.wait_ge(sS.h, s_rstd)
            return sV.v

        with ExitStack() as sAB:
            eAB = sAB.enter_context
            z = eAB(nc.sbuf_tensor("z", [P, DT * T], F32))

            sIN.inc(SP.dma_start(out=spar[:], in_=par), 16)
            par_ld = sIN.v

            sV.inc(V.memset(ones_bc[:], 1.0))
            sV.inc(V.memset(epsc[:], EPS))
            sV.inc(V.memset(mhalf[:], -0.5))

            with ExitStack() as sX:
                eX = sX.enter_context
                x = eX(nc.sbuf_tensor("x", [P, DT * T], F32))
                xbf = eX(nc.sbuf_tensor("xbf", [P, DT * T], BF16))

                xt_r = xt.rearrange("(dt p) t -> p dt t", p=P)
                sIN.inc(SP.dma_start(
                    out=x[:].rearrange("p (dt t) -> p dt t", t=T),
                    in_=xt_r), 16)
                x_ld = sIN.v
                sWD.inc(GP.dma_start(
                    out=xbf[:].rearrange("p (dt t) -> p dt t", t=T),
                    in_=xt_r), 16)
                xbf_ld = sWD.v

                V.wait_ge(sWD.h, xbf_ld)
                V.wait_ge(sIN.h, x_ld)
                for ft in range(T // FT):
                    emit_ln_stats_ft(
                        lambda dt, ft=ft: ("res", xbf[:, dt * T + ft * FT:
                                                      dt * T + ft * FT + FT]))
                    for dt in range(DT):
                        xap = x[:, dt * T + ft * FT: dt * T + ft * FT + FT]
                        zap = z[:, dt * T + ft * FT: dt * T + ft * FT + FT]
                        V.tensor_tensor(out=zap, in0=xap, in1=m_bc[:],
                                        op=ALU.subtract)
                        V.tensor_tensor(out=zap, in0=zap, in1=r_bc[:],
                                        op=ALU.mult)
                z_done = sV.inc(V.last())
            dbg_dump(SP, "z", z[:].rearrange("p (dt t) -> p dt t", t=T),
                     (sV, z_done))

            V_DTS = range(0, 6)
            P_DTS = range(6, 8)
            VA, VB = 6, 8
            with ExitStack() as sTab:
                eTb = sTab.enter_context
                ktab = eTb(nc.sbuf_tensor("ktab", [P, DT * 16], F32))
                ubf = eTb(nc.sbuf_tensor("ubf", [P, DT * 256], BF16))
                pvbf = eTb(nc.sbuf_tensor("pvbf", [P, DT * 256], BF16))
                mbf = eTb(nc.sbuf_tensor("mbf", [P, DT * 256], BF16))
                _tV = eTb(nc.sbuf_tensor("tmpV0", [P, 2048], F32))
                _tP = eTb(nc.sbuf_tensor("tmpP0", [P, 2048], F32))
                tmpV = [_tV, _tV]
                tmpP = [_tP, _tP]
                hcur = eTb(nc.sbuf_tensor("hcur", [P, 2 * 128], BF16))
                redv = eTb(nc.sbuf_tensor("redv", [P, 128], BF16))
                corr2 = eTb(nc.sbuf_tensor("corr2", [P, 4 * 256], BF16))
                htg = eTb(nc.sbuf_tensor("htg", [P, 4 * 256], BF16))
                zbf = eTb(nc.sbuf_tensor("zbf", [P, DT * TQ], BF16))

                kv = ktab[:].rearrange("p (dt j) -> p dt j", dt=DT)
                ubfv = ubf[:].rearrange("p (dt t n) -> p dt t n", dt=DT, t=16)
                pvbfv = pvbf[:].rearrange("p (dt n t) -> p dt n t", dt=DT,
                                          n=16)
                mbv = mbf[:].rearrange("p (dt n m) -> p dt n m", dt=DT, n=16)

                with ExitStack() as sT2:
                    eT2 = sT2.enter_context
                    sparA = eT2(nc.sbuf_tensor("sparA", [P, DT * 256], F32))
                    utab32 = eT2(nc.sbuf_tensor("utab32", [P, DT * 272], F32))
                    pvt32 = eT2(nc.sbuf_tensor("pvt32", [P, DT * 256], F32))
                    Abf = eT2(nc.sbuf_tensor("Abf", [P, DT * 256], BF16))
                    lvlA = eT2(nc.sbuf_tensor("lvlA", [P, DT * 256], BF16))
                    lvlB = eT2(nc.sbuf_tensor("lvlB", [P, DT * 256], BF16))
                    srcT = eT2(nc.sbuf_tensor("srcT", [P, DT * 256], BF16))

                    SP.wait_ge(sV.h, z_done)
                    sIN.inc(SP.dma_start(out=sparA[:], in_=parA), 16)
                    sparA_ld = sIN.v
                    V.wait_ge(sIN.h, sparA_ld)
                    S.wait_ge(sIN.h, sparA_ld)
                    S.wait_ge(sV.h, z_done)
                    S.low_precision[0] = True
                    ins = S.copy(out=Abf[:], in_=sparA[:])
                    S.low_precision[0] = False
                    abf_done = sS.inc(ins)

                    uv17 = utab32[:].rearrange("p (dt r n) -> p dt r n",
                                               dt=DT, r=17)
                    pvv = pvt32[:].rearrange("p (dt t n) -> p dt t n",
                                             dt=DT, t=16)
                    cmv = pcol("Cm", 0, 128).rearrange("p (dt n) -> p dt n",
                                                       dt=DT)
                    bmv = pcol("Bm", 0, 128).rearrange("p (dt n) -> p dt n",
                                                       dt=DT)
                    V.tensor_copy(out=uv17[:, :, 0, :], in_=cmv)
                    V.tensor_copy(out=pvv[:, :, 15, :], in_=bmv)
                    A_nm = sparA[:].rearrange("p (dt n m) -> p dt n m",
                                              dt=DT, n=16)
                    A_T = A_nm.transpose([0, 1, 3, 2])
                    tv0 = tmpV[0][:].rearrange("p (dt a b) -> p dt a b",
                                               dt=DT, a=16)
                    tv1 = tmpV[1][:].rearrange("p (dt a b) -> p dt a b",
                                               dt=DT, a=16)
                    for j in range(16):
                        V.tensor_tensor(
                            out=tv0,
                            in0=uv17[:, :, j, :].unsqueeze(2)
                            .broadcast_to([P, DT, 16, 16]),
                            in1=A_T, op=ALU.mult)
                        V.tensor_reduce(out=uv17[:, :, j + 1, :], in_=tv0,
                                        axis=AX.X, op=ALU.add)
                        if j < 15:
                            V.tensor_tensor(
                                out=tv1,
                                in0=pvv[:, :, 15 - j, :].unsqueeze(2)
                                .broadcast_to([P, DT, 16, 16]),
                                in1=A_nm, op=ALU.mult)
                            V.tensor_reduce(out=pvv[:, :, 14 - j, :],
                                            in_=tv1, axis=AX.X, op=ALU.add)
                    V.tensor_tensor(
                        out=tv0, in0=uv17[:, :, 0:16, :],
                        in1=bmv.unsqueeze(2).broadcast_to([P, DT, 16, 16]),
                        op=ALU.mult)
                    V.tensor_reduce(out=kv, in_=tv0, axis=AX.X, op=ALU.add)
                    V.tensor_tensor(out=kv[:, :, 0:1], in0=kv[:, :, 0:1],
                                    in1=pcol("Dm", 0, DT).unsqueeze(2),
                                    op=ALU.add)
                    V.low_precision[0] = True
                    V.tensor_copy(out=ubfv, in_=uv17[:, :, 1:17, :])
                    ins = V.tensor_copy(out=pvbfv,
                                        in_=pvv.transpose([0, 1, 3, 2]))
                    V.low_precision[0] = False
                    uvdone = sV.inc(ins)

                    GP.wait_ge(sV.h, z_done)
                    GP.wait_ge(sS.h, abf_done)
                    V.wait_ge(sS.h, abf_done)
                    chain = [Abf, lvlA, lvlB, lvlA, mbf]
                    v_done_lvl = 0
                    p_done_lvl = 0
                    for lvl in range(4):
                        sv = chain[lvl][:].rearrange(
                            "p (dt n m) -> p dt n m", dt=DT, n=16)
                        dv = chain[lvl + 1][:].rearrange(
                            "p (dt n m) -> p dt n m", dt=DT, n=16)
                        stv = srcT[:].rearrange("p (dt m k) -> p dt m k",
                                                dt=DT, m=16)
                        if lvl:
                            V.wait_ge(sGP.h, p_done_lvl)
                            GP.wait_ge(sV.h, v_done_lvl)
                        V.low_precision[0] = True
                        ins = V.tensor_copy(out=stv,
                                            in_=sv.transpose([0, 1, 3, 2]))
                        V.low_precision[0] = False
                        tcp = sV.inc(ins)
                        GP.wait_ge(sV.h, tcp)
                        for eng, T2, dts, lp in ((V, tmpV, V_DTS, V),
                                                 (GP, tmpP, P_DTS, GP)):
                            for i, dt in enumerate(dts):
                                tb = T2[i % 2][:].bitcast(BF16).rearrange(
                                    "p (n m k) -> p n m k", n=16, m=16)
                                lp.low_precision[0] = True
                                eng.tensor_tensor(
                                    out=tb,
                                    in0=sv[:, dt].unsqueeze(2)
                                    .broadcast_to([P, 16, 16, 16]),
                                    in1=stv[:, dt].unsqueeze(1)
                                    .broadcast_to([P, 16, 16, 16]),
                                    op=ALU.mult)
                                red16(eng, dv[:, dt], tb)
                                lp.low_precision[0] = False
                        v_done_lvl = sV.inc(V.last())
                        p_done_lvl = sGP.inc(GP.last())
                    for nm, tbuf in (("M", mbf), ("pv", pvbf)):
                        if nm in dbg:
                            SP.wait_ge(sV.h, v_done_lvl)
                            SP.wait_ge(sGP.h, p_done_lvl)
                            ins = SP.dma_start(out=dbg[nm], in_=tbuf[:])
                            done = sOUT.inc(ins, 16)
                            V.wait_ge(sOUT.h, done)

                Y = eTb(nc.sbuf_tensor("Y", [P, DT * TQ], F32))
                Hb = eTb(nc.sbuf_tensor("Hb", [P, DT * 16 * 33], BF16))
                g1 = eTb(nc.sbuf_tensor("g1", [P, DT * TQ], BF16))
                wo_ch = eTb(nc.sbuf_tensor("wo_ch", [P, 2 * DT * P], BF16))

                wo_cons = [0, 0]
                wo_wb = [0, 0]
                bank_rr = [0]

                def next_bank():
                    b = bank_rr[0]
                    bank_rr[0] = (b + 1) % 4
                    return b

                def hbv_of(dt):
                    return Hb[:, dt * 528: dt * 528 + 528].rearrange(
                        "p (n c) -> p n c", n=16)

                hball = Hb[:].rearrange("p (dt n c) -> p dt n c", dt=DT,
                                        n=16)
                hc = hcur[:].rearrange("p (pp d n) -> p pp d n", pp=2, d=DT)
                rv = redv[:].rearrange("p (d n) -> p d n", d=DT)
                GP.wait_ge(sV.h, uvdone)

                ppg = [0]
                prev_g1_all = [0]
                prev_innerV = [0]
                prev_hle_p = [0]
                prev_wout = [0]
                for q in range(NQ):
                    hb = q * TQ
                    if prev_hle_p[0]:
                        S.wait_ge(sGP.h, prev_hle_p[0])
                    S.low_precision[0] = True
                    ins = S.copy(
                        out=zbf[:].rearrange("p (dt t) -> p dt t", t=TQ),
                        in_=z[:].rearrange("p (dt t) -> p dt t", t=T)
                        [:, :, hb:hb + TQ])
                    S.low_precision[0] = False
                    zbf_done = sS.inc(ins)

                    if prev_g1_all[0]:
                        V.wait_ge(sS.h, prev_g1_all[0])
                        GP.wait_ge(sS.h, prev_g1_all[0])
                        GP.wait_ge(sV.h, prev_innerV[0])
                    for eng, dts in ((V, V_DTS), (GP, P_DTS)):
                        for dt in dts:
                            zf = z[:, dt * T + hb: dt * T + hb + TQ]
                            zv = zf.rearrange("p (c t) -> p c t", t=L)
                            yf = Y[:, dt * TQ: dt * TQ + TQ]
                            yv = yf.rearrange("p (c t) -> p c t", t=L)
                            if eng is V:
                                eng.tensor_scalar(
                                    out=yf, in0=zf,
                                    scalar1=ktab[:, dt * 16:dt * 16 + 1],
                                    scalar2=None, op0=ALU.mult)
                                pass
                                for j in range(1, L):
                                    eng.scalar_tensor_tensor(
                                        out=yv[:, :, j:L],
                                        in0=zv[:, :, 0:L - j],
                                        scalar=ktab[:, dt * 16 + j:
                                                    dt * 16 + j + 1],
                                        in1=yv[:, :, j:L], op0=ALU.mult,
                                        op1=ALU.add)
                            else:
                                eng.tensor_tensor(
                                    out=yf, in0=zf,
                                    in1=ktab[:, dt * 16:dt * 16 + 1]
                                    .broadcast_to([P, TQ]),
                                    op=ALU.mult)
                                tsc = tmpP[0][:, :CHQ * L]
                                tv = tsc.rearrange("p (c t) -> p c t", t=L)
                                for j in range(1, L):
                                    eng.tensor_tensor(
                                        out=tv[:, :, 0:L - j],
                                        in0=zv[:, :, 0:L - j],
                                        in1=ktab[:, dt * 16 + j:
                                                 dt * 16 + j + 1]
                                        .unsqueeze(2)
                                        .broadcast_to([P, CHQ, L - j]),
                                        op=ALU.mult)
                                    eng.tensor_tensor(
                                        out=yv[:, :, j:L],
                                        in0=yv[:, :, j:L],
                                        in1=tv[:, :, 0:L - j], op=ALU.add)

                    V.wait_ge(sS.h, zbf_done)
                    GP.wait_ge(sS.h, zbf_done)
                    zb = zbf[:].rearrange("p (dt c t) -> p dt c t", dt=DT,
                                          t=L)
                    for eng, T2, dts in ((V, tmpV, V_DTS), (GP, tmpP, P_DTS)):
                        for dt in dts:
                            for g in range(CHQ // 16):
                                tb = T2[g % 2][:].bitcast(BF16).rearrange(
                                    "p (n c t) -> p n c t", n=16, c=16)
                                eng.low_precision[0] = True
                                eng.tensor_tensor(
                                    out=tb,
                                    in0=zb[:, dt, 16 * g:16 * g + 16, :]
                                    .unsqueeze(1)
                                    .broadcast_to([P, 16, 16, L]),
                                    in1=pvbfv[:, dt].unsqueeze(2)
                                    .broadcast_to([P, 16, 16, L]),
                                    op=ALU.mult)
                                red16(eng, hball[:, dt, :,
                                                 1 + 16 * g: 17 + 16 * g],
                                      tb)
                                eng.low_precision[0] = False
                    if q == 0 and "Hle0" in dbg:
                        hle_v = sV.inc(V.last())
                        hle_p = sGP.inc(GP.last())
                        SP.wait_ge(sV.h, hle_v)
                        SP.wait_ge(sGP.h, hle_p)
                        ins = SP.dma_start(out=dbg["Hle0"], in_=Hb[:])
                        done = sOUT.inc(ins, 16)
                        V.wait_ge(sOUT.h, done)

                    c0 = 1 if q == 0 else 0
                    hle_p = sGP.inc(GP.last())
                    prev_hle_p[0] = hle_p
                    V.wait_ge(sGP.h, hle_p)
                    dsls = ((0, 3), (3, 6), (6, 8))
                    pp = ppg[0]
                    V.low_precision[0] = True
                    for a, b in dsls:
                        if q == 0:
                            V.tensor_copy(
                                out=hc[:, pp, a:b, :].unsqueeze(3),
                                in_=hball[:, a:b, :, 1:2])
                        else:
                            V.tensor_copy(
                                out=hball[:, a:b, :, 0:1],
                                in_=hc[:, pp, a:b, :].unsqueeze(3))
                    V.low_precision[0] = False
                    V.wait_ge(sV.h, sV.inc(V.last()))
                    for c in range(c0, CHQ):
                        V.low_precision[0] = True
                        for k2, (a, b) in enumerate(dsls):
                            nd = b - a
                            tb = tmpV[k2 % 2][:].bitcast(BF16) \
                                [:, :nd * 256].rearrange(
                                    "p (d n m) -> p d n m", d=nd, n=16)
                            V.tensor_tensor(
                                out=tb, in0=mbv[:, a:b],
                                in1=hc[:, pp, a:b, :].unsqueeze(2)
                                .broadcast_to([P, nd, 16, 16]),
                                op=ALU.mult)
                            red16(V, rv[:, a:b, :], tb)
                        for a, b in dsls:
                            V.tensor_tensor(
                                out=hc[:, 1 - pp, a:b, :].unsqueeze(3),
                                in0=rv[:, a:b, :].unsqueeze(3),
                                in1=hball[:, a:b, :, c + 1:c + 2],
                                op=ALU.add)
                        for a, b in dsls:
                            V.tensor_tensor(
                                out=hball[:, a:b, :, c + 1:c + 2],
                                in0=hball[:, a:b, :, c + 1:c + 2],
                                in1=rv[:, a:b, :].unsqueeze(3),
                                op=ALU.add)
                        V.low_precision[0] = False
                        pp = 1 - pp
                    ppg[0] = pp
                    bnd_v = sV.inc(V.last())
                    GP.wait_ge(sV.h, bnd_v)

                    for ei, (eng, T2, dts, sE) in enumerate(
                            ((V, tmpV, V_DTS, sV), (GP, tmpP, P_DTS, sGP))):
                        for dt in dts:
                            yv = Y[:, dt * TQ: dt * TQ + TQ].rearrange(
                                "p (c t) -> p c t", t=L)
                            gis = list(enumerate(range(c0, CHQ, 16)))
                            eng.low_precision[0] = True
                            for gi, cl in gis:
                                w = min(16, CHQ - cl)
                                sl = (2 * ei + gi % 2) * 256
                                ht = htg[:, sl:sl + w * 16].rearrange(
                                    "p (c n) -> p c n", c=w)
                                eng.tensor_copy(
                                    out=ht,
                                    in_=hball[:, dt, :, cl:cl + w]
                                    .transpose([0, 2, 1]))
                            eng.low_precision[0] = False
                            eng.wait_ge(sE.h, sE.inc(eng.last()))
                            eng.low_precision[0] = True
                            for gi, cl in gis:
                                w = min(16, CHQ - cl)
                                sl = (2 * ei + gi % 2) * 256
                                ht = htg[:, sl:sl + w * 16].rearrange(
                                    "p (c n) -> p c n", c=w)
                                tb = T2[gi % 2][:].bitcast(BF16) \
                                    [:, :16 * w * 16].rearrange(
                                        "p (t c n) -> p t c n", t=16, c=w)
                                eng.tensor_tensor(
                                    out=tb,
                                    in0=ubfv[:, dt].unsqueeze(2)
                                    .broadcast_to([P, 16, w, 16]),
                                    in1=ht.unsqueeze(1)
                                    .broadcast_to([P, 16, w, 16]),
                                    op=ALU.mult)
                                cb = corr2[:, sl:sl + 16 * w].rearrange(
                                    "p (t c) -> p t c", t=16)
                                red16(eng, cb, tb)
                            eng.low_precision[0] = False
                            eng.wait_ge(sE.h, sE.inc(eng.last()))
                            for gi, cl in gis:
                                w = min(16, CHQ - cl)
                                sl = (2 * ei + gi % 2) * 256
                                cb = corr2[:, sl:sl + 16 * w].rearrange(
                                    "p (t c) -> p t c", t=16)
                                eng.tensor_tensor(
                                    out=yv[:, cl:cl + w, :],
                                    in0=yv[:, cl:cl + w, :],
                                    in1=cb.transpose([0, 2, 1]), op=ALU.add)
                    pool_half = sGP.inc(GP.last())
                    V.wait_ge(sGP.h, pool_half)
                    S.wait_ge(sGP.h, pool_half)
                    y_done = sV.inc(V.last())
                    if q == 0 and "Hb0" in dbg:
                        SP.wait_ge(sV.h, y_done)
                        ins = SP.dma_start(out=dbg["Hb0"], in_=Hb[:])
                        done = sOUT.inc(ins, 16)
                        V.wait_ge(sOUT.h, done)
                    dbg_dump(SP, f"Y{q}",
                             Y[:].rearrange("p (dt t) -> p dt t", t=TQ),
                             (sV, y_done))

                    S.wait_ge(sV.h, y_done)
                    if prev_wout[0]:
                        S.wait_ge(sT.h, prev_wout[0])
                    g1_done = {}
                    for ft in range(TQ // FT):
                        emit_ln_stats_ft(
                            lambda dt, ft=ft: ("copy",
                                               Y[:, dt * TQ + ft * FT:
                                                 dt * TQ + ft * FT + FT]))
                        for dt in range(DT):
                            yap = Y[:, dt * TQ + ft * FT:
                                    dt * TQ + ft * FT + FT]
                            tsc = acts[:, (dt % 2) * FT:(dt % 2) * FT + FT]
                            if (ft, dt) >= (0, 2):
                                key = (ft, dt - 2) if dt >= 2 else \
                                    (ft - 1, DT - 2 + dt)
                                V.wait_ge(sS.h, g1_done[key])
                            V.tensor_tensor(out=tsc, in0=yap, in1=m_bc[:],
                                            op=ALU.subtract)
                            ins = V.tensor_tensor(out=tsc, in0=tsc,
                                                  in1=r_bc[:], op=ALU.mult)
                            vv = sV.inc(ins)
                            S.wait_ge(sV.h, vv)
                            ins = S.activation(
                                out=g1[:, dt * TQ + ft * FT:
                                       dt * TQ + ft * FT + FT],
                                in_=tsc, func=AF.Gelu,
                                scale=pcol("lnw", dt, 1),
                                bias=pcol("lnb", dt, 1))
                            g1_done[(ft, dt)] = sS.inc(ins)
                    g1_all = g1_done[(TQ // FT - 1, DT - 1)]
                    prev_g1_all[0] = g1_all
                    prev_innerV[0] = sV.v
                    if q == 0:
                        dbg_dump(SP, "g10",
                                 g1[:].rearrange("p (dt t) -> p dt t", t=TQ),
                                 (sS, g1_all))

                    for ob in range(DT):
                        slot = ob % 2
                        chf = wo_ch[:, slot * DT * P: slot * DT * P + DT * P]
                        ch = chf.rearrange("p (it o) -> p it o", it=DT)
                        if wo_cons[slot]:
                            GP.wait_ge(sT.h, wo_cons[slot])
                        if wo_wb[slot]:
                            GP.wait_ge(sWB.h, wo_wb[slot])
                        if q == 0:
                            sWD.inc(GP.dma_start(
                                out=ch, in_=wout_t[:, ob * P:ob * P + P]
                                .rearrange("(it p) o -> p it o", p=P)), 16)
                            ch_ld = sWD.v
                            SP.wait_ge(sWD.h, ch_ld)
                            sWB.inc(SP.dma_start(
                                out=woutb[ob * P:(ob + 1) * P, :],
                                in_=chf), 16)
                            wo_wb[slot] = sWB.v
                        else:
                            sWD.inc(GP.dma_start(
                                out=chf,
                                in_=woutb[ob * P:(ob + 1) * P, :]), 16)
                            ch_ld = sWD.v
                        for ftl in range(TQ // FT):
                            bank = next_bank()
                            TT.wait_ge(sWD.h, ch_ld)
                            TT.wait_ge(sS.h, g1_all)
                            acquire_bank(bank)
                            for it in range(DT):
                                ins = TT.matmul(
                                    out=pb[bank][:], lhsT=ch[:, it, :],
                                    rhs=g1[:, it * TQ + ftl * FT:
                                           it * TQ + ftl * FT + FT],
                                    start=(it == 0), stop=(it == DT - 1))
                            tmm = sT.inc(ins)
                            V.wait_ge(sT.h, tmm)
                            zap = z[:, ob * T + hb + ftl * FT:
                                    ob * T + hb + ftl * FT + FT]
                            ins = V.scalar_tensor_tensor(
                                out=zap, in0=pb[bank][:],
                                scalar=pcol("bout", ob, 1), in1=zap,
                                op0=ALU.add, op1=ALU.add)
                            bank_free[bank] = (sV, sV.inc(ins))
                        wo_cons[slot] = sT.v
                    prev_wout[0] = sT.v
                a_done = sV.v
            dbg_dump(SP, "a", z[:].rearrange("p (dt t) -> p dt t", t=T),
                     (sV, a_done))

            with ExitStack() as sXX:
                eXX = sXX.enter_context
                x0s = eXX(nc.sbuf_tensor("x0s", [P, DT * TQ], F32))
                x1s = eXX(nc.sbuf_tensor("x1s", [P, DT * TQ], F32))
                for th in range(NQ):
                    if th >= 1:
                        SP.wait_ge(sV.h, sV.v)
                    SP.wait_ge(sV.h, a_done)
                    sIN.inc(SP.dma_start(
                        out=x0s[:].rearrange("p (dt t) -> p dt t", t=TQ),
                        in_=xt.rearrange("(dt p) t -> p dt t", p=P)
                        [:, :, th * TQ:th * TQ + TQ]), 16)
                    x0_ld = sIN.v
                    S.wait_ge(sV.h, a_done)
                    for ft in range(TQ // FT):
                        emit_ln_stats_ft(
                            lambda dt, ft=ft, th=th:
                            ("copy", z[:, dt * T + th * TQ + ft * FT:
                                       dt * T + th * TQ + ft * FT + FT]))
                        V.wait_ge(sIN.h, x0_ld)
                        for dt in range(DT):
                            aap = z[:, dt * T + th * TQ + ft * FT:
                                    dt * T + th * TQ + ft * FT + FT]
                            tsc = acts[:, (dt % 2) * FT:(dt % 2) * FT + FT]
                            x0ap = x0s[:, dt * TQ + ft * FT:
                                       dt * TQ + ft * FT + FT]
                            x1ap = x1s[:, dt * TQ + ft * FT:
                                       dt * TQ + ft * FT + FT]
                            V.tensor_tensor(out=tsc, in0=aap, in1=m_bc[:],
                                            op=ALU.subtract)
                            V.tensor_tensor(out=tsc, in0=tsc, in1=r_bc[:],
                                            op=ALU.mult)
                            V.tensor_tensor(out=x1ap, in0=tsc, in1=x0ap,
                                            op=ALU.add)
                    x1_done = sV.inc(V.last())
                    SP.wait_ge(sV.h, x1_done)
                    sOUT.inc(SP.dma_start(
                        out=x1sp.rearrange("(dt p) t -> p dt t", p=P)
                        [:, :, th * TQ:th * TQ + TQ],
                        in_=x1s[:].rearrange("p (dt t) -> p dt t", t=TQ)), 16)
        x1_spilled = sOUT.v
        if "x1" in dbg:
            SP.wait_ge(sOUT.h, x1_spilled)
            sOUT.inc(SP.dma_start(out=dbg["x1"], in_=x1sp), 16)
            x1_spilled = sOUT.v

        with ExitStack() as sFF:
            eF = sFF.enter_context
            x1t = eF(nc.sbuf_tensor("x1t", [P, DT * TH], F32))
            hbuf = eF(nc.sbuf_tensor("hbuf", [P, OT4 * TH], BF16))
            ch1 = eF(nc.sbuf_tensor("ch1", [P, 2 * DT * P], BF16))
            ch2 = eF(nc.sbuf_tensor("ch2", [P, 2 * DT * P], BF16))
            CH3W = OT4 * P
            ch3 = eF(nc.sbuf_tensor("ch3", [P, 2 * CH3W], BF16))

            ch_cons = {"w12": [0, 0], "c3": [0, 0]}
            tmpa_guard = [0, 0]
            bank_rr = [0]
            GP.wait_ge(sV.h, sV.v)
            GP.wait_ge(sS.h, sS.v)
            SP.wait_ge(sV.h, sV.v)
            SP.wait_ge(sS.h, sS.v)

            def next_bank():
                b = bank_rr[0]
                bank_rr[0] = (b + 1) % 4
                return b

            x1t_reader_v = 0
            for th in range(2):
                SP.wait_ge(sOUT.h, x1_spilled)
                if x1t_reader_v:
                    SP.wait_ge(sV.h, x1t_reader_v)
                sIN.inc(SP.dma_start(
                    out=x1t[:].rearrange("p (dt t) -> p dt t", t=TH),
                    in_=x1sp.rearrange("(dt p) t -> p dt t", p=P)
                    [:, :, th * TH:th * TH + TH]), 16)
                x1_ld = sIN.v

                with ExitStack() as sZ2:
                    z2t = sZ2.enter_context(
                        nc.sbuf_tensor(f"z2t{th}", [P, DT * TH], BF16))
                    S.wait_ge(sIN.h, x1_ld)
                    if th == 1:
                        V.wait_ge(sOUT.h, sOUT.v)
                        S.wait_ge(sOUT.h, sOUT.v)
                    for ft in range(TH // FT):
                        emit_ln_stats_ft(
                            lambda dt, ft=ft: ("copy",
                                               x1t[:, dt * TH + ft * FT:
                                                   dt * TH + ft * FT + FT]))
                        for dt in range(DT):
                            xap = x1t[:, dt * TH + ft * FT:
                                      dt * TH + ft * FT + FT]
                            tsc = acts[:, (dt % 2) * FT:(dt % 2) * FT + FT]
                            zap = z2t[:, dt * TH + ft * FT:
                                      dt * TH + ft * FT + FT]
                            V.tensor_tensor(out=tsc, in0=xap, in1=m_bc[:],
                                            op=ALU.subtract)
                            V.tensor_tensor(out=zap, in0=tsc, in1=r_bc[:],
                                            op=ALU.mult)
                    z2_done = sV.inc(V.last())
                    if th == 0:
                        dbg_dump(SP, "z20",
                                 z2t[:].rearrange("p (dt t) -> p dt t", t=TH),
                                 (sV, z2_done))

                    for ob in range(OT4):
                        slot = ob % 2
                        c1 = ch1[:, slot * DT * P:(slot + 1) * DT * P] \
                            .rearrange("p (it o) -> p it o", it=DT)
                        c2 = ch2[:, slot * DT * P:(slot + 1) * DT * P] \
                            .rearrange("p (it o) -> p it o", it=DT)
                        if ch_cons["w12"][slot]:
                            GP.wait_ge(sT.h, ch_cons["w12"][slot])
                        sWD.inc(GP.dma_start(
                            out=c1, in_=w1_t[:, ob * P:ob * P + P]
                            .rearrange("(it p) o -> p it o", p=P)), 16)
                        sWD.inc(GP.dma_start(
                            out=c2, in_=w2_t[:, ob * P:ob * P + P]
                            .rearrange("(it p) o -> p it o", p=P)), 16)
                        ch_ld = sWD.v
                        for ftl in range(TH // FT):
                            b1 = next_bank()
                            b2 = next_bank()
                            TT.wait_ge(sWD.h, ch_ld)
                            TT.wait_ge(sV.h, z2_done)
                            acquire_bank(b1)
                            acquire_bank(b2)
                            for it in range(DT):
                                TT.matmul(out=pb[b1][:], lhsT=c1[:, it, :],
                                          rhs=z2t[:, it * TH + ftl * FT:
                                                  it * TH + ftl * FT + FT],
                                          start=(it == 0),
                                          stop=(it == DT - 1))
                            for it in range(DT):
                                ins = TT.matmul(
                                    out=pb[b2][:], lhsT=c2[:, it, :],
                                    rhs=z2t[:, it * TH + ftl * FT:
                                            it * TH + ftl * FT + FT],
                                    start=(it == 0), stop=(it == DT - 1))
                            tmm = sT.inc(ins)
                            S.wait_ge(sT.h, tmm)
                            if tmpa_guard[ftl % 2]:
                                S.wait_ge(sV.h, tmpa_guard[ftl % 2])
                            tmpa = acts[:, (ftl % 2) * FT:(ftl % 2) * FT + FT]
                            ins = S.activation(out=tmpa, in_=pb[b1][:],
                                               func=AF.Silu,
                                               scale=pcol("s1", ob, 1))
                            sse = sS.inc(ins)
                            bank_free[b1] = (sS, sse)
                            V.wait_ge(sS.h, sse)
                            hap = hbuf[:, ob * TH + ftl * FT:
                                       ob * TH + ftl * FT + FT]
                            ins = V.scalar_tensor_tensor(
                                out=hap, in0=pb[b2][:],
                                scalar=pcol("s2", ob, 1), in1=tmpa,
                                op0=ALU.mult, op1=ALU.mult)
                            vval = sV.inc(ins)
                            bank_free[b2] = (sV, vval)
                            tmpa_guard[ftl % 2] = vval
                        ch_cons["w12"][slot] = sT.v
                    h_done = sV.v
                if th == 0:
                    dbg_dump(SP, "h0",
                             hbuf[:].rearrange("p (ot t) -> p ot t", t=TH),
                             (sV, h_done))

                with ExitStack() as sG:
                    gbuf = sG.enter_context(
                        nc.sbuf_tensor(f"gbuf{th}", [P, OT4 * TH], BF16))

                    def big_mm_phase(wsrc, rhs_buf, n_ob, scale_name, dst_fn,
                                     rhs_ready, act_func):
                        for ob in range(n_ob):
                            slot = ob % 2
                            cc = ch3[:, slot * CH3W:
                                     slot * CH3W + CH3W].rearrange(
                                "p (it o) -> p it o", it=OT4)
                            if ch_cons["c3"][slot]:
                                GP.wait_ge(sT.h, ch_cons["c3"][slot])
                            sWD.inc(GP.dma_start(
                                out=cc, in_=wsrc[:, ob * P:ob * P + P]
                                .rearrange("(it p) o -> p it o", p=P)), 16)
                            ch_ld = sWD.v
                            for ftl in range(TH // FT):
                                bank = next_bank()
                                TT.wait_ge(sWD.h, ch_ld)
                                TT.wait_ge(rhs_ready[0].h, rhs_ready[1])
                                acquire_bank(bank)
                                for it in range(OT4):
                                    ins = TT.matmul(
                                        out=pb[bank][:], lhsT=cc[:, it, :],
                                        rhs=rhs_buf[:, it * TH + ftl * FT:
                                                    it * TH + ftl * FT + FT],
                                        start=(it == 0),
                                        stop=(it == OT4 - 1))
                                tmm = sT.inc(ins)
                                S.wait_ge(sT.h, tmm)
                                ins = S.activation(
                                    out=dst_fn(ob, ftl), in_=pb[bank][:],
                                    func=act_func,
                                    scale=pcol(scale_name, ob, 1))
                                bank_free[bank] = (sS, sS.inc(ins))
                            ch_cons["c3"][slot] = sT.v
                        return sS.v

                    g_done = big_mm_phase(
                        h1_t, hbuf, OT4, "s1h",
                        lambda ob, ftl: gbuf[:, ob * TH + ftl * FT:
                                             ob * TH + ftl * FT + FT],
                        (sV, h_done), AF.Gelu)
                    if th == 0:
                        dbg_dump(SP, "g0",
                                 gbuf[:].rearrange("p (ot t) -> p ot t",
                                                   t=TH),
                                 (sS, g_done))
                    k_done = big_mm_phase(
                        h2_t, gbuf, OT4, "s2h",
                        lambda ob, ftl: hbuf[:, ob * TH + ftl * FT:
                                             ob * TH + ftl * FT + FT],
                        (sS, g_done), AF.Gelu)
                    fbuf = gbuf[:].bitcast(F32)
                    f_done = big_mm_phase(
                        w3_t, hbuf, DT, "s3",
                        lambda ob, ftl: fbuf[:, ob * TH + ftl * FT:
                                             ob * TH + ftl * FT + FT],
                        (sS, k_done), AF.Copy)
                    if th == 0:
                        dbg_dump(SP, "f0",
                                 fbuf[:, 0:DT * TH].rearrange(
                                     "p (dt t) -> p dt t", t=TH),
                                 (sS, f_done))

                    S.wait_ge(sS.h, f_done)
                    for ft in range(TH // FT):
                        emit_ln_stats_ft(
                            lambda dt, ft=ft: ("copy",
                                               fbuf[:, dt * TH + ft * FT:
                                                    dt * TH + ft * FT + FT]))
                        for dt in range(DT):
                            fap = fbuf[:, dt * TH + ft * FT:
                                       dt * TH + ft * FT + FT]
                            tsc = acts[:, (dt % 2) * FT:(dt % 2) * FT + FT]
                            x1ap = x1t[:, dt * TH + ft * FT:
                                       dt * TH + ft * FT + FT]
                            V.tensor_tensor(out=tsc, in0=fap, in1=m_bc[:],
                                            op=ALU.subtract)
                            V.tensor_tensor(out=tsc, in0=tsc, in1=r_bc[:],
                                            op=ALU.mult)
                            ins = V.tensor_tensor(out=fap, in0=tsc, in1=x1ap,
                                                  op=ALU.add)
                            vdone = sV.inc(ins)
                            SP.wait_ge(sV.h, vdone)
                            sOUT.inc(SP.dma_start(
                                out=y_out.rearrange("(dt p) t -> p dt t",
                                                    p=P)
                                [:, dt, th * TH + ft * FT:
                                 th * TH + ft * FT + FT],
                                in_=fap), 16)
                    x1t_reader_v = sV.v
            SP.wait_ge(sOUT.h, sOUT.v)

        with nc.Block() as block:
            @block.sync
            def _(e):
                _replay(nc, SP, e)

            @block.gpsimd
            def _(e):
                _replay(nc, GP, e)

            @block.vector
            def _(e):
                _replay(nc, V, e)

            @block.scalar
            def _(e):
                _replay(nc, S, e)

            @block.tensor
            def _(e):
                _replay(nc, TT, e)
    return nc


_HOST_CACHE = {}


def _prep_shared(inp):
    def t(a):
        return np.ascontiguousarray(a.T)

    A = inp["L0_attn_A"].reshape(D, 256)
    parA = np.ascontiguousarray(
        A.reshape(DT, P, 256).transpose(1, 0, 2).reshape(P, DT * 256))

    def cols(a, w):
        return a.reshape(w, P).T

    par = np.concatenate([
        inp["L0_attn_Bm"].reshape(DT, P, 16).transpose(1, 0, 2).reshape(P, 128),
        inp["L0_attn_Cm"].reshape(DT, P, 16).transpose(1, 0, 2).reshape(P, 128),
        cols(inp["L0_attn_Dm"], DT),
        cols(inp["L0_attn_ln_w"], DT),
        cols(inp["L0_attn_ln_b"], DT),
        cols(inp["L0_attn_b_out"], DT),
        cols(inp["L0_ffn_w1_s"], OT4),
        cols(inp["L0_ffn_w2_s"], OT4),
        cols(inp["L0_ffn_h1_s"], OT4),
        cols(inp["L0_ffn_h2_s"], OT4),
        cols(inp["L0_ffn_w3_s"], DT),
    ], axis=1)
    par = np.ascontiguousarray(par.astype(np.float32))
    assert par.shape == (P, NPAR)

    return {
        "parA": parA.astype(np.float32),
        "par": par,
        "wout_t": t(inp["L0_attn_W_out"]),
        "w1_t": t(inp["L0_ffn_w1_w"]),
        "w2_t": t(inp["L0_ffn_w2_w"]),
        "h1_t": t(inp["L0_ffn_h1_w"]),
        "h2_t": t(inp["L0_ffn_h2_w"]),
        "w3_t": t(inp["L0_ffn_w3_w"]),
    }


def kernel(**inputs):
    x = inputs["L0_attn_x"]
    B = x.shape[0]
    shared = _prep_shared(inputs)
    nc = build_nc()
    in_maps = []
    for b in range(B):
        m = dict(shared)
        m["xt"] = np.ascontiguousarray(x[b].T)
        in_maps.append(m)
    res = run_bass_kernel_spmd(nc, in_maps, core_ids=list(range(B)))
    out = np.stack([res.results[b]["y_out"].T for b in range(B)], axis=0)
    return np.ascontiguousarray(out.astype(np.float32))
